# version 12 (final)
# baseline (speedup 1.0000x reference)
import numpy as np
import ml_dtypes
import concourse.bass as bass
import concourse.mybir as mybir
from concourse.bass_utils import run_bass_kernel_spmd

F32 = mybir.dt.float32
BF16 = mybir.dt.bfloat16
AF = mybir.ActivationFunctionType
ALU = mybir.AluOpType
AX = mybir.AxisListType

MAGIC = 12582912.0
D = 1024
S = 2048
NH = 16
DHEAD = 64
HID = 4096
DC = D // 128
HC = HID // 128
NT = S // 512
EPS = 1e-5

LAUNCH_PLAN = [[0, 1, 2, 3]]

BF = ml_dtypes.bfloat16


class KB:
    ENGS = ("sync", "gpsimd", "tensor", "vector", "scalar")
    SEMS = ("pe", "dve", "act")

    def __init__(self, nc):
        self.nc = nc
        self.sched = {e: [] for e in self.ENGS}
        self.cnt = {s: 0 for s in self.SEMS}
        self.last_wait = {}
        self.sem_h = None

    def _flat(self, dep, out):
        if dep is None:
            return
        if isinstance(dep, (list, tuple)) and dep and isinstance(dep[0], (list, tuple, type(None))):
            for d in dep:
                self._flat(d, out)
            return
        out.append(dep)

    def wait(self, eng, dep):
        flat = []
        self._flat(dep, flat)
        by_sem = {}
        for sem, val in flat:
            if val > by_sem.get(sem, 0):
                by_sem[sem] = val
        for sem, val in by_sem.items():
            k = (eng, sem)
            if self.last_wait.get(k, 0) >= val:
                continue
            self.last_wait[k] = val
            self.sched[eng].append(lambda e, s=sem, v=val: e.wait_ge(self.sem_h[s], v))

    def op(self, eng, fn, sem=None, amt=1):
        if eng in ("vector", "scalar"):
            own = "dve" if eng == "vector" else "act"
            self.wait(eng, (own, self.cnt.get(own, 0)))
        if sem is None:
            self.sched[eng].append(fn)
            return None
        if sem not in self.cnt:
            self.cnt[sem] = 0
        self.cnt[sem] += amt
        v = self.cnt[sem]
        self.sched[eng].append(
            lambda e, f=fn, s=sem, a=amt: f(e).then_inc(self.sem_h[s], a)
        )
        return (sem, v)

    def dma(self, queue, out, in_, deps=None, sem=None, **kw):
        self.wait(queue, deps)
        if sem is None:
            sem = "ld" if queue == "sync" else "gp"
        if sem not in self.cnt:
            self.cnt[sem] = 0
        return self.op(queue, lambda e: e.dma_start(out=out, in_=in_, **kw), sem, 16)

    def now(self, sem):
        return (sem, self.cnt.get(sem, 0))

    def barrier(self):
        snap = [(s, v) for s, v in self.cnt.items() if v > 0]
        for e in self.ENGS:
            for d in snap:
                self.wait(e, d)


class Carve:
    def __init__(self, nc, total_bytes=206 * 1024):
        self.total = total_bytes
        self.mega = nc.alloc_sbuf_tensor("mega", [128, total_bytes // 2], BF16)
        self.mega_f32 = self.mega.bitcast(F32).reshape([128, total_bytes // 4])
        self.perm_off = 0

    def bf(self, byte_off, cols):
        assert byte_off % 32 == 0 and byte_off + 2 * cols <= self.total
        return self.mega[:, byte_off // 2 : byte_off // 2 + cols]

    def f32(self, byte_off, cols):
        assert byte_off % 32 == 0 and byte_off + 4 * cols <= self.total
        return self.mega_f32[:, byte_off // 4 : byte_off // 4 + cols]


OFF_STREAM = 0
OFF_CONST = 65536
OFF_ROWS = OFF_CONST + 2048
OFF_SMALL = OFF_ROWS + 8192
OFF_PHASE = OFF_SMALL + 2048
PHASE_BYTES = 206 * 1024 - OFF_PHASE


class Views:

    def __init__(self, cv: Carve):
        self.cv = cv
        self.stream = cv.f32(OFF_STREAM, 16384)
        self.tril = cv.bf(OFF_CONST, 128)
        self.id128 = cv.bf(OFF_CONST + 256, 128)
        self.ones_bf = cv.bf(OFF_CONST + 512, 1)
        self.magic = cv.f32(OFF_CONST + 544, 1)
        self.epsc = cv.f32(OFF_CONST + 576, 1)
        self.ones1x64 = cv.f32(OFF_CONST + 608, 64)
        self.ones1x128 = cv.f32(OFF_CONST + 864, 128)
        self.rows = cv.f32(OFF_ROWS, 2048)
        self.small = cv.f32(OFF_SMALL, 512)

    def row(self, i, width=512):
        return self.rows[0:1, i * 512 : i * 512 + width]

    def stream_sl(self, dc, c0, c1):
        return self.stream[:, dc * 2048 + c0 : dc * 2048 + c1]


def stream_tile(v, dc, qt):
    return v.stream[:, dc * 2048 + qt * 512 : dc * 2048 + (qt + 1) * 512]


def emit_ln(kb: KB, v: Views, ps, r_dep):
    nc = kb.nc
    psBig = ps["big"]
    rbf = ps["lnbuf0"]
    sqbf = ps["lnbuf1"]
    tmpf = ps["lntmp"]
    for qt in range(NT):
        stat_deps = []
        prev_cast = None
        for dc in range(DC):
            sl = stream_tile(v, dc, qt)
            kb.wait("vector", r_dep)
            kb.wait("vector", prev_cast)
            c1 = kb.op("vector", lambda e, o=rbf, i=sl: e.tensor_copy(o, i), "dve")
            kb.wait("scalar", r_dep)
            kb.wait("scalar", prev_cast)
            c2 = kb.op(
                "scalar",
                lambda e, o=sqbf, i=sl: e.activation(o, i, AF.Square),
                "act",
            )
            kb.wait("tensor", c1)
            m1 = kb.op(
                "tensor",
                lambda e, o=psBig[0:1, 0:512], l=v.ones_bf, r=rbf, s=(dc == 0), t=(
                    dc == DC - 1
                ): e.matmul(o, l, r, start=s, stop=t),
                "pe",
            )
            kb.wait("tensor", c2)
            m2 = kb.op(
                "tensor",
                lambda e, o=psBig[0:1, 512:1024], l=v.ones_bf, r=sqbf, s=(dc == 0), t=(
                    dc == DC - 1
                ): e.matmul(o, l, r, start=s, stop=t),
                "pe",
            )
            prev_cast = [m1, m2]
            stat_deps = [m1, m2]
        r_mean = v.row(0)
        r_msq = v.row(1)
        r_rstd = v.row(2)
        r_bneg = v.row(3)
        kb.wait("vector", stat_deps)
        d1 = kb.op(
            "vector",
            lambda e: e.tensor_scalar_mul(r_mean, psBig[0:1, 0:512], 1.0 / D),
            "dve",
        )
        d2 = kb.op(
            "vector",
            lambda e: e.tensor_scalar_mul(r_msq, psBig[0:1, 512:1024], 1.0 / D),
            "dve",
        )
        d3 = kb.op(
            "vector", lambda e: e.tensor_mul(r_bneg, r_mean, r_mean), "dve"
        )
        d4 = kb.op(
            "vector", lambda e: e.tensor_sub(r_msq, r_msq, r_bneg), "dve"
        )
        kb.wait("scalar", d4)
        d5 = kb.op(
            "scalar",
            lambda e: e.activation(
                r_rstd, r_msq, AF.Sqrt, bias=v.epsc[0:1, 0:1], scale=1.0
            ),
            "act",
        )
        kb.wait("vector", d5)
        d6 = kb.op("vector", lambda e: e.reciprocal(r_rstd, r_rstd), "dve")
        kb.wait("vector", d6)
        d7 = kb.op("vector", lambda e: e.tensor_mul(r_bneg, r_mean, r_rstd), "dve")
        d8 = kb.op(
            "vector", lambda e: e.tensor_scalar_mul(r_bneg, r_bneg, -1.0), "dve"
        )
        kb.wait("tensor", [d6, d8, d1, d2])
        b1 = kb.op(
            "tensor",
            lambda e: e.matmul(
                psBig[:, 0:512], v.ones1x128[0:1, :], r_rstd, start=True, stop=True
            ),
            "pe",
        )
        b2 = kb.op(
            "tensor",
            lambda e: e.matmul(
                psBig[:, 512:1024], v.ones1x128[0:1, :], r_bneg, start=True, stop=True
            ),
            "pe",
        )
        app_deps = []
        for dc in range(DC):
            sl = stream_tile(v, dc, qt)
            kb.wait("vector", [b1, b2])
            a1 = kb.op(
                "vector",
                lambda e, s=sl: e.tensor_mul(tmpf, s, psBig[:, 0:512]),
                "dve",
            )
            kb.wait("vector", a1)
            a2 = kb.op(
                "vector",
                lambda e, s=sl: e.tensor_add(s, tmpf, psBig[:, 512:1024]),
                "dve",
            )
            app_deps.append(a2)
        kb.wait("tensor", app_deps)
    return kb.now("dve")


def emit_l0_attn(kb: KB, v: Views, dram, ps, x_dep):
    nc = kb.nc
    cv = v.cv
    A = OFF_PHASE
    xbf = cv.bf(A, 16384)
    vtm = cv.bf(A + 32768, 16640)
    zfm = cv.bf(A + 66048, 16384)
    kqb = cv.bf(A + 98816, 4096)
    ws = cv.bf(A + 107008, 8192)
    wkq = cv.bf(A + 123392, 2048)
    exb = cv.bf(A + 127488, 2048)
    ps["lnbuf0"] = cv.bf(A + 98816, 512)
    ps["lnbuf1"] = cv.bf(A + 99840, 512)
    ps["lntmp"] = cv.f32(A + 100864, 512)

    pAB, pCD = ps["pAB"], ps["pCD"]
    p5 = ps["p5"]
    psBig = ps["big"]
    b4 = ps["b4"]

    xbf_deps = []
    for dc in range(DC):
        d = kb.dma(
            "gpsimd", xbf[:, dc * 2048 : (dc + 1) * 2048],
            dram["xin"][dc * 128 : (dc + 1) * 128, :], sem="ga",
        )
        xbf_deps.append(d)
    wv_deps = []
    for dc in range(DC):
        d = kb.dma(
            "gpsimd", ws[:, dc * 1024 : (dc + 1) * 1024],
            dram["l0_wvT"][dc * 128 : (dc + 1) * 128, :], sem="gb",
        )
        wv_deps.append(d)

    vtm4 = vtm.rearrange("p (a h e) -> p a h e", a=16, h=16)
    ones_dep = kb.op("vector", lambda e: e.memset(vtm4[:, :, :, 64:65], 1.0), "dve")

    vtm_deps = [ones_dep]
    prev_evac = [None, None, None, None]
    for pt in range(16):
        for fh in range(2):
            bi = (pt * 2 + fh) % 4
            pb = b4[bi]
            kb.wait("tensor", xbf_deps)
            kb.wait("tensor", wv_deps)
            kb.wait("tensor", prev_evac[bi])
            mm = None
            for dc in range(DC):
                mm = kb.op(
                    "tensor",
                    lambda e, o=pb, l=xbf[:, dc * 2048 + pt * 128 : dc * 2048 + pt * 128 + 128],
                    r=ws[:, dc * 1024 + fh * 512 : dc * 1024 + fh * 512 + 512],
                    s=(dc == 0), t=(dc == DC - 1): e.matmul(o, l, r, start=s, stop=t),
                    "pe",
                )
            out_ap = vtm4[:, pt, fh * 8 : (fh + 1) * 8, 0:64]
            kb.wait("vector", mm)
            ev = kb.op(
                "vector",
                lambda e, o=out_ap, i=pb: e.tensor_copy(
                    o, i.rearrange("p (h e) -> p h e", h=8)
                ),
                "dve",
            )
            prev_evac[bi] = ev
            vtm_deps.append(ev)

    zfm_deps = []
    last_kq_use = None
    pend_n3 = [None, None]
    pend_n4 = [None]
    prev_n5 = [None, None]
    zbanks = [p5[0:65, 0:512], psBig[0:65, 1024:1536]]
    zpar = 0
    pend = []

    def flush_pend(all_=False):
        want = 0
        while len(pend) > want:
            pend.pop(0)()

    ex_guard = [None, None]
    sp_guard = [None, None]

    for hp in range(8):
        wk_d = []
        for dc in range(DC):
            d = kb.dma(
                "gpsimd", wkq[:, dc * 128 : dc * 128 + 128],
                dram["l0_wkT"][dc * 128 : (dc + 1) * 128, hp * 128 : hp * 128 + 128],
                deps=last_kq_use, sem="gw",
            )
            wk_d.append(d)
            d2 = kb.dma(
                "gpsimd", wkq[:, 1024 + dc * 128 : 1024 + dc * 128 + 128],
                dram["l0_wqT"][dc * 128 : (dc + 1) * 128, hp * 128 : hp * 128 + 128],
                sem="gw",
            )
            wk_d.append(d2)
        kq_deps = []
        for which in range(2):
            for tt in range(NT):
                bi = (which * NT + tt) % 4
                pb = b4[bi]
                kb.wait("tensor", wk_d)
                kb.wait("tensor", xbf_deps)
                kb.wait("tensor", prev_evac[bi])
                mm = None
                for dc in range(DC):
                    mm = kb.op(
                        "tensor",
                        lambda e, o=pb,
                        l=wkq[:, which * 1024 + dc * 128 : which * 1024 + dc * 128 + 128],
                        r=xbf[:, dc * 2048 + tt * 512 : dc * 2048 + tt * 512 + 512],
                        s=(dc == 0), t=(dc == DC - 1): e.matmul(o, l, r, start=s, stop=t),
                        "pe",
                    )
                out_ap = kqb[:, which * 2048 + tt * 512 : which * 2048 + tt * 512 + 512]
                kb.wait("scalar", mm)
                ev = kb.op(
                    "scalar",
                    lambda e, o=out_ap, i=pb: e.activation(o, i, AF.Identity),
                    "act",
                )
                prev_evac[bi] = ev
                kq_deps.append(ev)

        last_kq_use = kb.now("pe")

        for hl in range(2):
            head = 2 * hp + hl
            hr0 = 64 * hl
            for qt in range(NT):
                pz = zbanks[zpar]
                nch = 4 * qt + 4
                zmm = None
                for pc2 in range(nch // 2):
                    c0, c1 = 2 * pc2, 2 * pc2 + 1
                    rel0 = 128 * c0 - 512 * qt
                    rel1 = 128 * c1 - 512 * qt
                    n00 = max(rel0, 0)
                    n01 = max(rel1, 0)
                    par = pc2 % 2
                    spb = [pAB, pCD][par]
                    eb = exb[:, par * 1024 : par * 1024 + 1024]
                    kb.wait("tensor", kq_deps)
                    kb.wait("tensor", sp_guard[par])
                    kb.wait("tensor", prev_evac[2 * par])
                    kb.wait("tensor", prev_evac[2 * par + 1])
                    sm0 = kb.op(
                        "tensor",
                        lambda e, o=spb[:, n00:512],
                        l=kqb[hr0 : hr0 + 64, c0 * 128 : c0 * 128 + 128],
                        r=kqb[hr0 : hr0 + 64, 2048 + qt * 512 + n00 : 2048 + qt * 512 + 512]:
                        e.matmul(o, l, r, start=True, stop=True),
                        "pe",
                    )
                    sm1 = kb.op(
                        "tensor",
                        lambda e, o=spb[:, 512 + n01 : 1024],
                        l=kqb[hr0 : hr0 + 64, c1 * 128 : c1 * 128 + 128],
                        r=kqb[hr0 : hr0 + 64, 2048 + qt * 512 + n01 : 2048 + qt * 512 + 512]:
                        e.matmul(o, l, r, start=True, stop=True),
                        "pe",
                    )
                    if pc2 == 0:
                        flush_pend()
                    kb.wait("scalar", sm1)
                    kb.wait("scalar", ex_guard[par])
                    if rel0 >= 0:
                        e0 = kb.op(
                            "scalar",
                            lambda e, o=eb[:, n00:512], i=spb[:, n00:512]:
                            e.activation(o, i, AF.Exp, bias=0.0, scale=0.125),
                            "act",
                        )
                        e1 = kb.op(
                            "scalar",
                            lambda e, o=eb[:, 512 + n01 : 1024], i=spb[:, 512 + n01 : 1024]:
                            e.activation(o, i, AF.Exp, bias=0.0, scale=0.125),
                            "act",
                        )
                        kb.wait("vector", [e0, e1])
                        t0 = kb.op(
                            "vector",
                            lambda e, o=eb[:, rel0 : rel0 + 128]: e.tensor_mul(
                                o, o, v.tril
                            ),
                            "dve",
                        )
                        t1 = kb.op(
                            "vector",
                            lambda e, o=eb[:, 512 + rel1 : 512 + rel1 + 128]: e.tensor_mul(
                                o, o, v.tril
                            ),
                            "dve",
                        )
                        edep = [t0, t1]
                        sp_guard[par] = [e0, e1]
                    else:
                        ef = kb.op(
                            "scalar",
                            lambda e, o=eb, i=spb[:]: e.activation(
                                o, i, AF.Exp, bias=0.0, scale=0.125
                            ),
                            "act",
                        )
                        edep = ef
                        sp_guard[par] = ef
                    kb.wait("tensor", edep)
                    kb.wait("tensor", vtm_deps)
                    if pc2 == 0:
                        kb.wait("tensor", prev_n5[zpar])
                    z0 = kb.op(
                        "tensor",
                        lambda e, o=pz[:, n00:512],
                        l=vtm[:, c0 * 1040 + head * 65 : c0 * 1040 + head * 65 + 65],
                        r=eb[:, n00:512], s=(c0 == 0), t=False:
                        e.matmul(o, l, r, start=s, stop=t),
                        "pe",
                    )
                    z1 = kb.op(
                        "tensor",
                        lambda e, o=pz[:, n01:512],
                        l=vtm[:, c1 * 1040 + head * 65 : c1 * 1040 + head * 65 + 65],
                        r=eb[:, 512 + n01 : 1024], s=False, t=(c1 == nch - 1):
                        e.matmul(o, l, r, start=s, stop=t),
                        "pe",
                    )
                    zmm = z1
                    ex_guard[par] = z1
                zr = v.row(zpar)
                kb.wait("vector", zmm)
                kb.wait("vector", pend_n3[zpar])
                n1 = kb.op(
                    "vector", lambda e, i=pz, z=zr: e.tensor_copy(z, i[64:65, :]), "dve"
                )
                kb.wait("vector", n1)
                n2 = kb.op("vector", lambda e, z=zr: e.reciprocal(z, z), "dve")
                out_ap = zfm[hr0 : hr0 + 64,
                             hp * 2048 + qt * 512 : hp * 2048 + qt * 512 + 512]

                def mk_norm(n2=n2, pz=pz, out_ap=out_ap, zp=zpar, zr=zr):
                    def emitit():
                        kb.wait("tensor", n2)
                        kb.wait("tensor", pend_n4[0])
                        n3 = kb.op(
                            "tensor",
                            lambda e, z=zr: e.matmul(
                                psBig[0:64, 0:512], v.ones1x64[0:1, :], z,
                                start=True, stop=True,
                            ),
                            "pe",
                        )
                        kb.wait("vector", n3)
                        n4 = kb.op(
                            "vector",
                            lambda e: e.tensor_copy(
                                v.small[0:64, 0:512], psBig[0:64, 0:512]
                            ),
                            "dve",
                        )
                        kb.wait("vector", n4)
                        n5 = kb.op(
                            "vector",
                            lambda e, o=out_ap, i=pz: e.tensor_mul(
                                o, i[0:64, :], v.small[0:64, 0:512]
                            ),
                            "dve",
                        )
                        zfm_deps.append(n5)
                        pend_n3[zp] = n3
                        pend_n4[0] = n4
                        prev_n5[zp] = n5
                    return emitit

                pend.append(mk_norm())
                zpar ^= 1

    flush_pend(all_=True)

    wo_deps = []
    kb.wait("gpsimd", kb.now("pe"))
    for fc in range(DC):
        d = kb.dma(
            "gpsimd", ws[:, fc * 1024 : (fc + 1) * 1024],
            dram["l0_woT"][fc * 128 : (fc + 1) * 128, :], sem="gb",
        )
        wo_deps.append(d)
    res_deps = []
    for dc in range(DC):
        for qt in range(NT):
            bi = (dc * NT + qt) % 4
            pb = b4[bi]
            kb.wait("tensor", wo_deps)
            kb.wait("tensor", zfm_deps)
            kb.wait("tensor", prev_evac[bi])
            kb.wait("tensor", sp_guard[bi // 2])
            mm = None
            for fc in range(DC):
                mm = kb.op(
                    "tensor",
                    lambda e, o=pb,
                    l=ws[:, fc * 1024 + dc * 128 : fc * 1024 + dc * 128 + 128],
                    r=zfm[:, fc * 2048 + qt * 512 : fc * 2048 + qt * 512 + 512],
                    s=(fc == 0), t=(fc == DC - 1): e.matmul(o, l, r, start=s, stop=t),
                    "pe",
                )
            sl = stream_tile(v, dc, qt)
            kb.wait("vector", mm)
            kb.wait("vector", x_dep)
            ev = kb.op(
                "vector", lambda e, s=sl, i=pb: e.tensor_add(s, i, s), "dve"
            )
            prev_evac[bi] = ev
            res_deps.append(ev)

    kb.barrier()
    return emit_ln(kb, v, ps, res_deps)


def declare_dram(nc, blocks):
    dram = {}
    dram["xin"] = nc.dram_tensor("xin", [D, S], F32, kind="ExternalInput").ap()
    dram["xout"] = nc.dram_tensor("xout", [D, S], F32, kind="ExternalOutput").ap()
    dram["consts"] = nc.dram_tensor("consts", [128, 260], F32, kind="ExternalInput").ap()
    if 0 in blocks:
        dram["l0_wkT"] = nc.dram_tensor("l0_wkT", [D, D], F32, kind="ExternalInput").ap()
        dram["l0_wqT"] = nc.dram_tensor("l0_wqT", [D, D], F32, kind="ExternalInput").ap()
        dram["l0_wvT"] = nc.dram_tensor("l0_wvT", [D, D], F32, kind="ExternalInput").ap()
        dram["l0_woT"] = nc.dram_tensor("l0_woT", [D, D], F32, kind="ExternalInput").ap()
    if 1 in blocks:
        dram["l0_w1T"] = nc.dram_tensor("l0_w1T", [D, HID], F32, kind="ExternalInput").ap()
        dram["l0_w2T"] = nc.dram_tensor("l0_w2T", [HID, D], F32, kind="ExternalInput").ap()
        dram["l0_b1c"] = nc.dram_tensor("l0_b1c", [128, 32], F32, kind="ExternalInput").ap()
        dram["l0_b2c"] = nc.dram_tensor("l0_b2c", [128, 8], F32, kind="ExternalInput").ap()
        import os as _os
        _hk = "ExternalOutput" if _os.environ.get("KDBG_H") else "Internal"
        dram["hscratch"] = nc.dram_tensor("hscratch", [HID, S], BF16, kind=_hk).ap()
        if _os.environ.get("KDBG_H"):
            dram["dbg_xq"] = nc.dram_tensor("dbg_xq", [128, 4096], BF16, kind="ExternalOutput").ap()
            dram["dbg_sm"] = nc.dram_tensor("dbg_sm", [128, 512], F32, kind="ExternalOutput").ap()
            dram["dbg_w2"] = nc.dram_tensor("dbg_w2", [128, 32768], BF16, kind="ExternalOutput").ap()
            dram["dbg_r"] = nc.dram_tensor("dbg_r", [D, S], F32, kind="ExternalOutput").ap()
            dram["dbg_w1"] = nc.dram_tensor("dbg_w1", [128, 32768], BF16, kind="ExternalOutput").ap()

    if 2 in blocks:
        dram["l1_waT"] = nc.dram_tensor("l1_waT", [D, NH], F32, kind="ExternalInput").ap()
        dram["l1_wvT"] = nc.dram_tensor("l1_wvT", [D, D], F32, kind="ExternalInput").ap()
        dram["l1_woT"] = nc.dram_tensor("l1_woT", [D, D], F32, kind="ExternalInput").ap()
        dram["l1_bOc"] = nc.dram_tensor("l1_bOc", [128, 8], F32, kind="ExternalInput").ap()
        dram["wrev"] = nc.dram_tensor("wrev", [NH, 2560], BF16, kind="Internal").ap()
    if 3 in blocks:
        dram["l1_wexpA"] = nc.dram_tensor("l1_wexpA", [HID, D + 1], F32, kind="ExternalInput").ap()
        dram["l1_wconP"] = nc.dram_tensor("l1_wconP", [HID, D], F32, kind="ExternalInput").ap()
        dram["l1_upstk"] = nc.dram_tensor("l1_upstk", [32, 128, 1536], F32, kind="ExternalInput").ap()
        dram["l1_dnstk"] = nc.dram_tensor("l1_dnstk", [32, 128, 1536], F32, kind="ExternalInput").ap()
        dram["l1_ubc"] = nc.dram_tensor("l1_ubc", [128, 32], F32, kind="ExternalInput").ap()
        dram["l1_bcc"] = nc.dram_tensor("l1_bcc", [128, 8], F32, kind="ExternalInput").ap()
        dram["l1_dbc"] = nc.dram_tensor("l1_dbc", [128, 8], F32, kind="ExternalInput").ap()
        dram["cscr1"] = nc.dram_tensor("cscr1", [HID, D + 2], BF16, kind="Internal").ap()
        dram["cscr2"] = nc.dram_tensor("cscr2", [HID, D + 128], BF16, kind="Internal").ap()
        dram["vscr"] = nc.dram_tensor("vscr", [HID, D], BF16, kind="Internal").ap()
        dram["bscr"] = nc.dram_tensor("bscr", [32, 128], BF16, kind="Internal").ap()
        dram["gscratch"] = nc.dram_tensor("gscratch", [HID, S], BF16, kind="Internal").ap()
    return dram


def build_program(blocks):
    nc = bass.Bass("TRN2", target_bir_lowering=False)
    dram = declare_dram(nc, blocks)
    cv = Carve(nc)
    v = Views(cv)
    kb = KB(nc)

    import contextlib

    stack = contextlib.ExitStack()
    psd = {}
    psd["big"] = stack.enter_context(nc.psum_tensor("psBig", [128, 1536], F32))
    pAB = stack.enter_context(nc.psum_tensor("pAB", [128, 1024], F32))
    pCD = stack.enter_context(nc.psum_tensor("pCD", [128, 1024], F32))
    p5h = stack.enter_context(nc.psum_tensor("p5", [128, 512], F32))
    psd["pAB"] = pAB
    psd["pCD"] = pCD
    psd["p3"] = pAB[:, 0:512]
    psd["p4"] = pAB[:, 512:1024]
    psd["p5"] = p5h[:]
    psd["p6"] = psd["big"][:, 1024:1536]
    psd["b4"] = [pAB[:, 0:512], pAB[:, 512:1024], pCD[:, 0:512], pCD[:, 512:1024]]
    psd["pT"] = p5h.bitcast(BF16).reshape([128, 1024])[0:32, 0:128]


    cdep = []
    cdep.append(kb.dma("gpsimd", v.tril, dram["consts"][:, 0:128], sem="gc"))
    cdep.append(kb.dma("gpsimd", v.id128, dram["consts"][:, 128:256], sem="gc"))
    mset = [
        kb.op("vector", lambda e: e.memset(v.small, 0.0), "dve"),
        kb.op("vector", lambda e: e.memset(v.ones_bf, 1.0), "dve"),
        kb.op("vector", lambda e: e.memset(v.magic, MAGIC), "dve"),
        kb.op("vector", lambda e: e.memset(v.epsc, EPS), "dve"),
        kb.op("vector", lambda e: e.memset(v.ones1x64[0:1, :], 1.0), "dve"),
        kb.op("vector", lambda e: e.memset(v.ones1x128[0:1, :], 1.0), "dve"),
    ]
    kb.barrier()

    x_dep = []
    for dc in range(DC):
        d = kb.dma(
            "sync", v.stream[:, dc * 2048 : (dc + 1) * 2048],
            dram["xin"][dc * 128 : (dc + 1) * 128, :], sem="lx",
        )
        x_dep.append(d)

    dep = x_dep
    for b in blocks:
        if b == 0:
            dep = emit_l0_attn(kb, v, dram, psd, dep)
        elif b == 1:
            dep = emit_l0_ffn(kb, v, dram, psd, dep)
        elif b == 2:
            dep = emit_l1_attn(kb, v, dram, psd, dep)
        elif b == 3:
            dep = emit_l1_ffn(kb, v, dram, psd, dep)
        kb.barrier()

    st = []
    for dc in range(DC):
        d = kb.dma(
            "sync", dram["xout"][dc * 128 : (dc + 1) * 128, :],
            v.stream[:, dc * 2048 : (dc + 1) * 2048], sem="so",
        )
        st.append(d)
    kb.wait("sync", st)

    with stack:
        kb.sem_h = {
            name: stack.enter_context(nc.semaphore(name)) for name in kb.cnt
        }
        with nc.Block() as block:

            @block.sync
            def _(sync):
                for fn in kb.sched["sync"]:
                    fn(sync)

            @block.gpsimd
            def _(gpsimd):
                for fn in kb.sched["gpsimd"]:
                    fn(gpsimd)

            @block.tensor
            def _(tensor):
                for fn in kb.sched["tensor"]:
                    fn(tensor)

            @block.vector
            def _(vector):
                for fn in kb.sched["vector"]:
                    fn(vector)

            @block.scalar
            def _(scalar):
                for fn in kb.sched["scalar"]:
                    fn(scalar)

    return nc, dram


def emit_quant(kb, v, out_bf, in_f32, tmp_f32, scale, lo, hi, deps=None, relu=False):
    kb.wait("scalar", deps)
    s1 = kb.op(
        "scalar",
        lambda e: e.activation(tmp_f32, in_f32, AF.Identity, bias=v.magic[:, 0:1], scale=scale),
        "act",
    )
    kb.wait("vector", s1)
    lo_eff = 0.0 if relu else float(lo)
    s2 = kb.op(
        "vector",
        lambda e: e.tensor_scalar(tmp_f32, tmp_f32, MAGIC, lo_eff, ALU.subtract, ALU.max),
        "dve",
    )
    s3 = kb.op(
        "vector",
        lambda e: e.tensor_scalar(out_bf, tmp_f32, float(hi), 1.0 / scale, ALU.min, ALU.mult),
        "dve",
    )
    return s3


def emit_l0_ffn(kb, v, dram, ps, x_dep):
    cv = v.cv
    A = OFF_PHASE
    AW = cv.bf(A, 32768)
    WF = cv.f32(A + 65536, 4096)
    XQ = cv.bf(A + 81920, 4096)
    HL = cv.bf(A + 90112, 16384)
    HB = cv.bf(A + 122880, 1024)
    TQ = cv.f32(A + 124928, 1024)
    ps["lnbuf0"] = cv.bf(A + 129024, 512)
    ps["lnbuf1"] = cv.bf(A + 130048, 512)
    ps["lntmp"] = cv.f32(A + 124928, 512)
    bank = ps["b4"]
    hs = dram["hscratch"]

    b1 = v.small[:, 0:32]
    b2 = v.small[:, 32:40]
    b1q = v.small[:, 64:96]
    b2q = v.small[:, 128:136]
    bias1 = v.small[:, 192:224]
    d1 = kb.dma("sync", b1, dram["l0_b1c"], sem="lb0")
    d2 = kb.dma("sync", b2, dram["l0_b2c"], sem="lb0")
    qd = emit_quant(
        kb, v, TQ[:, 512:1024], v.small[:, 0:512], TQ[:, 0:512],
        64.0, -128, 127, deps=[d1, d2],
    )
    kb.wait("vector", qd)
    q1 = kb.op("vector", lambda e: e.tensor_copy(b1q, TQ[:, 512:544]), "dve")
    q2 = kb.op("vector", lambda e: e.tensor_copy(b2q, TQ[:, 544:552]), "dve")
    qb = kb.op(
        "vector",
        lambda e: e.tensor_scalar(TQ[:, 0:512], TQ[:, 512:1024], 16.0, MAGIC, ALU.mult, ALU.add),
        "dve",
    )
    kb.wait("vector", qb)
    q3 = kb.op("vector", lambda e: e.tensor_copy(bias1, TQ[:, 0:32]), "dve")

    if "dbg_sm" in dram:
        kb.wait("sync", q3)
        kb.dma("sync", dram["dbg_sm"], v.small, sem="dg")
    w1_deps = []
    prevh = [None, None]
    for i in range(2 * DC):
        dc, half = i // 2, i % 2
        slot = i % 2
        ld = kb.dma(
            "sync", WF[:, slot * 2048 : (slot + 1) * 2048],
            dram["l0_w1T"][dc * 128 : (dc + 1) * 128, half * 2048 : (half + 1) * 2048],
            deps=prevh[slot], sem=f"lw{slot}",
        )
        qd = emit_quant(
            kb, v, AW[:, dc * 4096 + half * 2048 : dc * 4096 + half * 2048 + 2048],
            WF[:, slot * 2048 : (slot + 1) * 2048],
            WF[:, slot * 2048 : (slot + 1) * 2048], 64.0, -128, 127, deps=ld,
        )
        prevh[slot] = qd
        w1_deps.append(qd)

    xqtmp = cv.f32(A + 90112, 512)
    h_store = []
    prev_hdma = [None, None]
    prev_psum = [None, None, None, None]
    prev_xq = None
    for tt in range(NT):
        xq_deps = []
        qprev = None
        for dc in range(DC):
            sl = v.stream[:, dc * 2048 + tt * 512 : dc * 2048 + tt * 512 + 512]
            kb.wait("scalar", x_dep)
            kb.wait("scalar", prev_xq)
            qd = emit_quant(
                kb, v, XQ[:, dc * 512 : (dc + 1) * 512], sl, xqtmp,
                16.0, -128, 127, deps=qprev,
            )
            qprev = qd
            xq_deps.append(qd)
        if tt == 0 and "dbg_xq" in dram:
            kb.dma("sync", dram["dbg_xq"], XQ, deps=xq_deps, sem="dg")
            kb.dma("sync", dram["dbg_w1"], AW, deps=w1_deps, sem="dg")
        for oc in range(HC):
            bi = oc % 4
            b2i = oc % 2
            pb = bank[bi]
            kb.wait("tensor", w1_deps)
            kb.wait("tensor", xq_deps)
            kb.wait("tensor", prev_psum[bi])
            mm = None
            for dc in range(DC):
                mm = kb.op(
                    "tensor",
                    lambda e, o=pb, l=AW[:, dc * 4096 + oc * 128 : dc * 4096 + oc * 128 + 128],
                    r=XQ[:, dc * 512 : (dc + 1) * 512],
                    s=(dc == 0), t=(dc == DC - 1): e.matmul(o, l, r, start=s, stop=t),
                    "pe",
                )
            kb.wait("scalar", mm)
            kb.wait("scalar", prev_hdma[b2i])
            s1 = kb.op(
                "scalar",
                lambda e, i=pb, o=TQ[:, b2i * 512 : b2i * 512 + 512],
                b=bias1[:, oc : oc + 1]: e.activation(
                    o, i, AF.Identity, bias=b, scale=16.0
                ),
                "act",
            )
            kb.wait("vector", s1)
            s2 = kb.op(
                "vector",
                lambda e, o=TQ[:, b2i * 512 : b2i * 512 + 512]: e.tensor_scalar(
                    o, o, MAGIC, 0.0, ALU.subtract, ALU.max
                ),
                "dve",
            )
            s3 = kb.op(
                "vector",
                lambda e, o=HB[:, b2i * 512 : b2i * 512 + 512],
                i=TQ[:, b2i * 512 : b2i * 512 + 512]: e.tensor_scalar(
                    o, i, 127.0, 1.0 / 16.0, ALU.min, ALU.mult
                ),
                "dve",
            )
            hd = kb.dma(
                "sync", hs[oc * 128 : (oc + 1) * 128, tt * 512 : tt * 512 + 512],
                HB[:, b2i * 512 : b2i * 512 + 512], deps=s3, sem=f"hs{b2i}",
            )
            prev_hdma[b2i] = hd
            prev_psum[bi] = s1
            h_store.append(hd)
        prev_xq = kb.now("pe")

    mm1_done = kb.now("pe")
    w2_deps = []
    prev_q = w1_deps[-1]
    for hc in range(HC):
        wslot = hc % 4
        ld = kb.dma(
            "sync", WF[:, wslot * 1024 : (wslot + 1) * 1024],
            dram["l0_w2T"][hc * 128 : (hc + 1) * 128, :], deps=prev_q,
            sem=f"lw{wslot}",
        )
        kb.wait("scalar", mm1_done)
        qd = emit_quant(
            kb, v, AW[:, hc * 1024 : (hc + 1) * 1024],
            WF[:, wslot * 1024 : (wslot + 1) * 1024],
            WF[:, wslot * 1024 : (wslot + 1) * 1024], 64.0, -128, 127, deps=ld,
        )
        if hc >= 3:
            prev_q = w2_deps[hc - 3]
        w2_deps.append(qd)

    if "dbg_w2" in dram:
        kb.dma("sync", dram["dbg_w2"], AW, deps=w2_deps, sem="dg")
    res_deps = []
    prev_ep = [prev_psum[0], prev_psum[1], prev_psum[2], prev_psum[3]]
    hl_prev = [kb.now("hs0"), kb.now("hs1")]
    for tt in range(NT):
        hl_deps = []
        for hc in range(HC):
            d = kb.dma(
                "sync", HL[:, hc * 512 : (hc + 1) * 512],
                hs[hc * 128 : (hc + 1) * 128, tt * 512 : tt * 512 + 512],
                deps=hl_prev, sem=f"hr{tt % 2}",
            )
            hl_deps.append(d)
        for dc in range(DC):
            bi = dc % 4
            pb = bank[bi]
            kb.wait("tensor", w2_deps)
            kb.wait("tensor", hl_deps)
            kb.wait("tensor", prev_ep[bi])
            mm = None
            for hc in range(HC):
                mm = kb.op(
                    "tensor",
                    lambda e, o=pb, l=AW[:, hc * 1024 + dc * 128 : hc * 1024 + dc * 128 + 128],
                    r=HL[:, hc * 512 : (hc + 1) * 512],
                    s=(hc == 0), t=(hc == HC - 1): e.matmul(o, l, r, start=s, stop=t),
                    "pe",
                )
            sl = v.stream[:, dc * 2048 + tt * 512 : dc * 2048 + tt * 512 + 512]
            kb.wait("vector", mm)
            kb.wait("vector", x_dep)
            ep = kb.op(
                "vector",
                lambda e, s=sl, i=pb, b=b2q[:, dc : dc + 1]: e.scalar_tensor_tensor(
                    s, i, b, s, ALU.add, ALU.add
                ),
                "dve",
            )
            prev_ep[bi] = ep
            res_deps.append(ep)
        hl_prev = kb.now("pe")

    if "dbg_r" in dram:
        for dc in range(DC):
            kb.dma("sync", dram["dbg_r"][dc * 128 : (dc + 1) * 128, :],
                   v.stream[:, dc * 2048 : (dc + 1) * 2048], deps=res_deps, sem="dg")
    return emit_ln(kb, v, ps, res_deps)


def rev_ap(ap):
    dims = [list(p) for p in ap.ap]
    assert len(dims) == 2 and dims[1][0] == 1
    (pstep, pcnt), (_, fcnt) = dims
    return bass.AP(ap.tensor, ap.offset + (fcnt - 1), [[pstep, pcnt], [-1, fcnt]])


def emit_l1_attn(kb, v, dram, ps, x_dep):
    cv = v.cv
    A = OFF_PHASE
    x2bf = cv.bf(A, 16384)
    vtm1 = cv.bf(A + 32768, 16384)
    yfm = cv.bf(A + 65536, 16384)
    z_sb = cv.f32(A + 65536, 2048)
    ez = cv.f32(A + 65536 + 8192, 2048)
    zp_t = cv.bf(A + 65536 + 16384, 2048)
    zq = cv.bf(A + 65536 + 20480, 2560)
    toep = [cv.bf(A + 98304, 2432), cv.bf(A + 98304 + 4864, 2432)]
    ws1 = cv.bf(A + 108032, 8192)
    waT = cv.bf(A + 124416, 128)
    ps["lnbuf0"] = cv.bf(A + 124672, 512)
    ps["lnbuf1"] = cv.bf(A + 125696, 512)
    ps["lntmp"] = cv.f32(A + 126720, 512)
    p5, p6 = ps["p5"], ps["p6"]
    bank = ps["b4"]
    zbank = [p5, p6]
    wrev = dram["wrev"]

    wv_deps = []
    for dc in range(DC):
        wv_deps.append(
            kb.dma("gpsimd", ws1[:, dc * 1024 : (dc + 1) * 1024],
                   dram["l1_wvT"][dc * 128 : (dc + 1) * 128, :], sem="gb")
        )
    wa_deps = []
    for dc in range(DC):
        wa_deps.append(
            kb.dma("gpsimd", waT[:, dc * 16 : (dc + 1) * 16],
                   dram["l1_waT"][dc * 128 : (dc + 1) * 128, :], sem="gw")
        )
    bO = v.small[:, 0:8]
    bo_dep = kb.dma("sync", bO, dram["l1_bOc"], sem="lb1")

    xb_deps = []
    for dc in range(DC):
        sl = v.stream[:, dc * 2048 : (dc + 1) * 2048]
        kb.wait("vector", x_dep)
        xb_deps.append(kb.op(
            "vector",
            lambda e, o=x2bf[:, dc * 2048 : (dc + 1) * 2048], i=sl: e.tensor_copy(o, i),
            "dve",
        ))

    prev_evac = [None, None, None, None]
    vtm_deps = []
    for pt in range(16):
        for fh in range(2):
            bi = (pt * 2 + fh) % 4
            pb = bank[bi]
            kb.wait("tensor", xb_deps)
            kb.wait("tensor", wv_deps)
            kb.wait("tensor", prev_evac[bi])
            mm = None
            for dc in range(DC):
                mm = kb.op(
                    "tensor",
                    lambda e, o=pb, l=x2bf[:, dc * 2048 + pt * 128 : dc * 2048 + pt * 128 + 128],
                    r=ws1[:, dc * 1024 + fh * 512 : dc * 1024 + fh * 512 + 512],
                    s=(dc == 0), t=(dc == DC - 1): e.matmul(o, l, r, start=s, stop=t),
                    "pe",
                )
            out_ap = vtm1[:, pt * 1024 + fh * 512 : pt * 1024 + fh * 512 + 512]
            kb.wait("vector", mm)
            ev = kb.op("vector", lambda e, o=out_ap, i=pb: e.tensor_copy(o, i), "dve")
            prev_evac[bi] = ev
            vtm_deps.append(ev)

    z_deps = []
    for tt in range(NT):
        bi = tt % 4
        pb = bank[bi]
        kb.wait("tensor", wa_deps)
        kb.wait("tensor", prev_evac[bi])
        mm = None
        for dc in range(DC):
            mm = kb.op(
                "tensor",
                lambda e, o=pb[0:16, :], l=waT[:, dc * 16 : (dc + 1) * 16],
                r=x2bf[:, dc * 2048 + tt * 512 : dc * 2048 + tt * 512 + 512],
                s=(dc == 0), t=(dc == DC - 1): e.matmul(o, l, r, start=s, stop=t),
                "pe",
            )
        kb.wait("vector", mm)
        ev = kb.op(
            "vector",
            lambda e, o=z_sb[0:16, tt * 512 : tt * 512 + 512], i=pb[0:16, :]:
            e.tensor_scalar_mul(o, i, 0.125),
            "dve",
        )
        prev_evac[bi] = ev
        z_deps.append(ev)

    mrow = v.small[0:16, 256:257]
    srow = v.small[0:16, 288:289]
    kb.wait("vector", z_deps)
    s1 = kb.op(
        "vector",
        lambda e: e.reduce_max(mrow, z_sb[0:16, 0:2048], axis=AX.X, negate=True),
        "dve",
    )
    kb.wait("scalar", s1)
    s2 = kb.op(
        "scalar",
        lambda e: e.activation(ez[0:16, 0:2048], z_sb[0:16, 0:2048], AF.Exp,
                               bias=mrow, scale=1.0),
        "act",
    )
    kb.wait("vector", s2)
    s3 = kb.op(
        "vector",
        lambda e: e.reduce_sum(srow, ez[0:16, 0:2048], axis=AX.X),
        "dve",
    )
    kb.wait("vector", s3)
    s4 = kb.op("vector", lambda e: e.reciprocal(srow, srow), "dve")
    kb.wait("vector", s4)
    s5 = kb.op(
        "vector",
        lambda e: e.tensor_scalar_mul(zp_t[0:16, 0:2048], ez[0:16, 0:2048], srow),
        "dve",
    )
    kb.wait("vector", s5)
    s6 = kb.op(
        "vector",
        lambda e: e.tensor_copy(zq[0:16, 0:2048], rev_ap(zp_t[0:16, 0:2048])),
        "dve",
    )
    s7 = kb.op("vector", lambda e: e.memset(zq[0:16, 2048:2560], 0.0), "dve")
    wz = kb.dma("sync", wrev, zq[0:16, 0:2560], deps=[s6, s7], sem="sz")

    yfm_deps = []
    tp_dep = [None, None]
    tp_ld = [None, None]
    prev_yev = [None, None]
    for head in range(NH):
        tb = head % 2
        tdeps = [kb.now("sz")]
        if tp_dep[tb] is not None:
            tdeps.append(tp_dep[tb])
        tld = kb.dma(
            "sync", toep[tb],
            bass.AP(wrev.tensor, head * 2560, [[1, 128], [1, 2432]]),
            deps=tdeps,
            sem=f"tp{tb}",
        )
        tp_ld[tb] = tld
        fc = head // 2
        r0 = (head % 2) * 64
        for tpp in range(NT):
            nch = NH - 4 * tpp
            zb = (head * NT + tpp) % 2
            pz = zbank[zb]
            kb.wait("tensor", tld)
            kb.wait("tensor", vtm_deps)
            kb.wait("tensor", prev_yev[zb])
            mm = None
            for c in range(nch):
                u0 = 512 * tpp + 128 * c
                mm = kb.op(
                    "tensor",
                    lambda e, o=pz[0:64, :], l=vtm1[:, c * 1024 + head * 64 : c * 1024 + head * 64 + 64],
                    r=toep[tb][:, u0 : u0 + 512],
                    s=(c == 0), t=(c == nch - 1): e.matmul(o, l, r, start=s, stop=t),
                    "pe",
                )
            out_sl = yfm[r0 : r0 + 64,
                         fc * 2048 + 512 * (3 - tpp) : fc * 2048 + 512 * (4 - tpp)]
            kb.wait("vector", mm)
            yev = kb.op(
                "vector",
                lambda e, o=rev_ap(out_sl), i=pz[0:64, :]: e.tensor_copy(o, i),
                "dve",
            )
            prev_yev[zb] = yev
            yfm_deps.append(yev)
        tp_dep[tb] = kb.now("pe")

    kb.wait("gpsimd", kb.now("pe"))
    wo_deps = []
    for fc2 in range(DC):
        wo_deps.append(
            kb.dma("gpsimd", ws1[:, fc2 * 1024 : (fc2 + 1) * 1024],
                   dram["l1_woT"][fc2 * 128 : (fc2 + 1) * 128, :], sem="gb")
        )
    res_deps = []
    for dc in range(DC):
        for qt in range(NT):
            bi = (dc * NT + qt) % 4
            pb = bank[bi]
            kb.wait("tensor", wo_deps)
            kb.wait("tensor", yfm_deps)
            kb.wait("tensor", prev_evac[bi])
            mm = None
            for fc2 in range(DC):
                mm = kb.op(
                    "tensor",
                    lambda e, o=pb,
                    l=ws1[:, fc2 * 1024 + dc * 128 : fc2 * 1024 + dc * 128 + 128],
                    r=yfm[:, fc2 * 2048 + qt * 512 : fc2 * 2048 + qt * 512 + 512],
                    s=(fc2 == 0), t=(fc2 == DC - 1): e.matmul(o, l, r, start=s, stop=t),
                    "pe",
                )
            sl = stream_tile(v, dc, qt)
            kb.wait("vector", mm)
            kb.wait("vector", bo_dep)
            ep = kb.op(
                "vector",
                lambda e, s=sl, i=pb, b=bO[:, dc : dc + 1]: e.scalar_tensor_tensor(
                    s, i, b, s, ALU.add, ALU.add
                ),
                "dve",
            )
            prev_evac[bi] = ep
            res_deps.append(ep)

    kb.barrier()
    return emit_ln(kb, v, ps, res_deps)


def emit_l1_ffn(kb, v, dram, ps, x_dep):
    cv = v.cv
    A = OFF_PHASE
    AW = cv.bf(A, 32768)
    XQb = cv.bf(A + 65536, 4096)
    GB = cv.bf(A + 73728, 1024)
    GL = cv.bf(A + 75776, 16384)
    B0 = A + 75776
    stg = [cv.bf(B0 + 3072 * i, 1536) for i in range(4)]
    zb = [cv.bf(B0 + 12288 + 256 * i, 128) for i in range(4)]
    wld = [cv.bf(B0 + 13312 + 2112 * i, 1028) for i in range(4)]
    wev = [cv.bf(B0 + 21760, 1028), cv.bf(B0 + 23872, 1028)]
    bcoll = cv.bf(B0 + 25984, 32)
    btr = cv.bf(B0 + 26080, 128)
    ps["lnbuf0"] = cv.bf(A + 108544, 512)
    ps["lnbuf1"] = cv.bf(A + 109568, 512)
    ps["lntmp"] = cv.f32(A + 110592, 512)
    psBig = ps["big"]
    pAB, pCD = ps["pAB"], ps["pCD"]
    p5 = ps["p5"]
    pT = ps["pT"]
    bank = ps["b4"]
    cscr1, cscr2, vscr, bscr, gs = (
        dram["cscr1"], dram["cscr2"], dram["vscr"], dram["bscr"], dram["gscratch"]
    )

    def stage_loads(j0, stk_ap, nstages, sidx):
        ss = (j0 // 2) % 2 * 2
        out = [None, None]
        for par in (0, 1):
            jj = j0 + par
            sl = ss + par
            src = stk_ap[jj, :, sidx * 128 : (sidx + nstages) * 128]
            out[par] = kb.dma(
                "gpsimd", stg[sl][:, 0 : nstages * 128], src,
                deps=self_guard[sl], sem=f"gs{sl}",
            )
        return out

    def chain_pair(j0, stk_ap, nstages, w_cols_in, wcols, sidx, sdeps, hasbias):
        ss = (j0 // 2) % 2 * 2
        zcur = [stg[ss][:, 0:128], stg[ss + 1][:, 0:128]]
        pdep = [chain_guard[0], chain_guard[1]]
        prod_ps = [pCD[:, 0:128], pCD[:, 512:640]]
        for i in range(1, nstages):
            for par in (0, 1):
                kb.wait("tensor", sdeps[par])
                kb.wait("tensor", pdep[par])
                mm = kb.op(
                    "tensor",
                    lambda e, l=stg[ss + par][:, i * 128 : (i + 1) * 128], r=zcur[par],
                    o=prod_ps[par]: e.matmul(o, l, r, start=True, stop=True),
                    "pe",
                )
                nz = zb[par * 2 + (i % 2)]
                kb.wait("vector", mm)
                ev = kb.op(
                    "vector",
                    lambda e, o=nz, i2=prod_ps[par]: e.tensor_copy(o, i2),
                    "dve",
                )
                pdep[par] = ev
                zcur[par] = nz
        self_guard[ss] = self_guard[ss + 1] = kb.now("pe")
        evs = [None, None]
        app_mm = [None, None]
        for par in (0, 1):
            kb.wait("tensor", pdep[par])
            kb.wait("tensor", wload_guard[par])
            kb.wait("tensor", apply_guard[par])
            mm = None
            wsl = wld[ss + par]
            if par == 0:
                for c0 in range(0, min(wcols, 1024), 512):
                    mm = kb.op(
                        "tensor",
                        lambda e, l=zcur[0], r=wsl[:, c0 : c0 + 512],
                        o=psBig[:, c0 : c0 + 512]: e.matmul(o, l, r, start=True, stop=True),
                        "pe",
                    )
                if hasbias:
                    mm = kb.op(
                        "tensor",
                        lambda e, l=zcur[0], r=wsl[:, 1024:1025],
                        o=psBig[:, 1024:1025]: e.matmul(o, l, r, start=True, stop=True),
                        "pe",
                    )
            else:
                for c0 in range(0, min(wcols, 1024), 512):
                    mm = kb.op(
                        "tensor",
                        lambda e, l=zcur[1], r=wsl[:, c0 : c0 + 512],
                        o=pAB[:, c0 : c0 + 512]: e.matmul(o, l, r, start=True, stop=True),
                        "pe",
                    )
                if hasbias:
                    mm = kb.op(
                        "tensor",
                        lambda e, l=zcur[1], r=wsl[:, 1024:1025],
                        o=p5[:, 0:1]: e.matmul(o, l, r, start=True, stop=True),
                        "pe",
                    )
            app_mm[par] = mm
        for par in (0, 1):
            kb.wait("vector", app_mm[par])
            kb.wait("vector", wev_guard[par])
            if par == 0:
                ev = kb.op(
                    "vector",
                    lambda e: e.tensor_copy(wev[0][:, 0:wcols], psBig[:, 0:wcols]),
                    "dve",
                )
            else:
                ev = kb.op(
                    "vector",
                    lambda e: e.tensor_copy(wev[1][:, 0:1024], pAB[:, 0:1024]),
                    "dve",
                )
                if hasbias:
                    ev = kb.op(
                        "vector",
                        lambda e: e.tensor_copy(wev[1][:, 1024:1025], p5[:, 0:1]),
                        "dve",
                    )
            evs[par] = ev
            apply_guard[par] = ev
            chain_guard[par] = pdep[par]
            wld_free[ss + par] = app_mm[par]
        return evs

    self_guard = [None, None, None, None]
    chain_guard = [None, None]
    apply_guard = [None, None]
    wload_guard = [None, None]
    wev_guard = [None, None]
    wld_free = [None, None, None, None]

    st_up = dram["l1_upstk"]

    def p1_loads(j):
        ss = (j // 2) % 2 * 2
        sd = stage_loads(j, st_up, 7, 0)
        wg = [None, None]
        for par in (0, 1):
            wg[par] = kb.dma(
                "gpsimd", wld[ss + par][:, 0:1025],
                dram["l1_wexpA"][(j + par) * 128 : (j + par + 1) * 128, :],
                deps=wld_free[ss + par], sem=f"gw{ss + par}",
            )
        return sd, wg

    pending = p1_loads(0)
    for j in range(0, HC, 2):
        sd, wg = pending
        wload_guard[0], wload_guard[1] = wg
        if j + 2 < HC:
            pending = p1_loads(j + 2)
        evs = chain_pair(j, st_up, 7, 1025, 1025, 0, sd, True)
        for par in (0, 1):
            st = kb.dma(
                "sync", cscr1[(j + par) * 128 : (j + par + 1) * 128, 0:1025],
                wev[par][:, 0:1025], deps=evs[par], sem=f"s1{par}",
            )
            wev_guard[par] = st

    c1r = cscr1.rearrange("(c p) n -> c p n", c=32)
    c2r = cscr2.rearrange("(c p) n -> c p n", c=32)
    bc_deps = []

    def p2_loads(j):
        ss = (j // 2) % 2 * 2
        sd = stage_loads(j, st_up, 5, 7)
        wg = [None, None]
        for par in (0, 1):
            jj = j + par
            ldp = [kb.now("s10"), kb.now("s11")]
            if wld_free[ss + par] is not None:
                ldp.append(wld_free[ss + par])
            wds = []
            for a in range(4):
                wds.append(kb.dma(
                    "sync", wld[ss + par][32 * a : 32 * a + 32, 0:1025],
                    c1r[:, 4 * jj + a, 0:1025], deps=ldp, sem=f"ga{ss + par}",
                ))
            wg[par] = wds
        return sd, wg

    pending = p2_loads(0)
    for j in range(0, HC, 2):
        sd, wg = pending
        wload_guard[0], wload_guard[1] = wg
        if j + 2 < HC:
            pending = p2_loads(j + 2)
        evs = chain_pair(j, st_up, 5, 1025, 1025, 7, sd, True)
        for par in (0, 1):
            jj = j + par
            kb.wait("vector", evs[par])
            bc = kb.op(
                "vector",
                lambda e, o=bcoll[:, jj : jj + 1], i=wev[par][:, 1024:1025]:
                e.tensor_copy(o, i),
                "dve",
            )
            bc_deps.append(bc)
            sts = []
            for a in range(4):
                sts.append(kb.dma(
                    "sync", c2r[:, 4 * jj + a, 0:1024],
                    wev[par][32 * a : 32 * a + 32, 0:1024], deps=evs[par],
                    sem=f"s2{par}",
                ))
            wev_guard[par] = sts

    kb.wait("tensor", bc_deps)
    bt = kb.op(
        "tensor",
        lambda e: e.transpose(pT[0:32, 0:128], bcoll[:, 0:32], v.id128),
        "pe",
    )
    kb.wait("vector", bt)
    bt2 = kb.op("vector", lambda e: e.tensor_copy(btr[0:32, :], pT[0:32, 0:128]), "dve")
    bt3 = kb.dma("sync", bscr, btr[0:32, 0:128], deps=bt2, sem="sb")
    borig = v.small[:, 64:96]
    ubc = v.small[:, 96:128]
    badd = v.small[:, 160:192]
    bd1 = kb.dma("gpsimd", borig, bass.AP(bscr.tensor, 0, [[32, 128], [1, 32]]),
                 deps=kb.now("sb"), sem="gc2")
    bd2 = kb.dma("sync", ubc, dram["l1_ubc"], sem="lb2")
    kb.wait("vector", [bd1, bd2])
    bd3 = kb.op("vector", lambda e: e.tensor_add(badd, borig, ubc), "dve")

    tw_deps = []
    for dc in range(DC):
        kb.wait("sync", [kb.now("s20"), kb.now("s21")])
        tw_deps.append(kb.op(
            "sync",
            lambda e, o=AW[:, dc * 4096 : (dc + 1) * 4096],
            i=cscr2[:, dc * 128 : (dc + 1) * 128]: e.dma_start_transpose(o, i),
            "tw", 16,
        ))

    st_dn = dram["l1_dnstk"]
    p5war = [kb.now("ga0"), kb.now("ga1")]

    def p5_loads(j):
        ss = (j // 2) % 2 * 2
        sd = stage_loads(j, st_dn, 5, 0)
        wg = [None, None]
        for par in (0, 1):
            wg[par] = kb.dma(
                "gpsimd", wld[ss + par][:, 0:1024],
                dram["l1_wconP"][(j + par) * 128 : (j + par + 1) * 128, :],
                deps=wld_free[ss + par], sem=f"gw{ss + par}",
            )
        return sd, wg

    pending = p5_loads(0)
    for j in range(0, HC, 2):
        sd, wg = pending
        wload_guard[0], wload_guard[1] = wg
        if j + 2 < HC:
            pending = p5_loads(j + 2)
        evs = chain_pair(j, st_dn, 5, 1024, 1024, 0, sd, False)
        for par in (0, 1):
            st = kb.dma(
                "sync", cscr1[(j + par) * 128 : (j + par + 1) * 128, 0:1024],
                wev[par][:, 0:1024], deps=[evs[par]] + p5war, sem=f"s3{par}",
            )
            wev_guard[par] = st

    def p6_loads(j):
        ss = (j // 2) % 2 * 2
        sd = stage_loads(j, st_dn, 7, 5)
        wg = [None, None]
        for par in (0, 1):
            jj = j + par
            ldp = [kb.now("s30"), kb.now("s31")]
            if wld_free[ss + par] is not None:
                ldp.append(wld_free[ss + par])
            wg[par] = kb.dma(
                "sync", wld[ss + par][:, 0:1024],
                bass.AP(cscr1.tensor, jj * (D + 2), [[32 * (D + 2), 128], [1, 1024]]),
                deps=ldp, sem=f"ga{ss + par}",
            )
        return sd, wg

    pending = p6_loads(0)
    for j in range(0, HC, 2):
        sd, wg = pending
        wload_guard[0], wload_guard[1] = wg
        if j + 2 < HC:
            pending = p6_loads(j + 2)
        evs = chain_pair(j, st_dn, 7, 1024, 1024, 5, sd, False)
        for par in (0, 1):
            st = kb.dma(
                "sync", vscr[(j + par) * 128 : (j + par + 1) * 128, :],
                wev[par][:, 0:1024], deps=evs[par], sem=f"sv{par}",
            )
            wev_guard[par] = st

    g_store = []
    prev_gdma = [None, None]
    prev_psum7 = [None, None, None, None]
    prev_xq = None
    for tt in range(NT):
        xb_deps = []
        for dc in range(DC):
            sl = v.stream[:, dc * 2048 + tt * 512 : dc * 2048 + tt * 512 + 512]
            kb.wait("vector", x_dep)
            kb.wait("vector", prev_xq)
            xb_deps.append(kb.op(
                "vector",
                lambda e, o=XQb[:, dc * 512 : (dc + 1) * 512], i=sl: e.tensor_copy(o, i),
                "dve",
            ))
        for oc in range(HC):
            bi = oc % 4
            b2i = oc % 2
            pb = bank[bi]
            kb.wait("tensor", tw_deps)
            kb.wait("tensor", xb_deps)
            kb.wait("tensor", prev_psum7[bi])
            mm = None
            for dc in range(DC):
                mm = kb.op(
                    "tensor",
                    lambda e, o=pb, l=AW[:, dc * 4096 + oc * 128 : dc * 4096 + oc * 128 + 128],
                    r=XQb[:, dc * 512 : (dc + 1) * 512],
                    s=(dc == 0), t=(dc == DC - 1): e.matmul(o, l, r, start=s, stop=t),
                    "pe",
                )
            kb.wait("scalar", mm)
            kb.wait("scalar", prev_gdma[b2i])
            kb.wait("scalar", bd3)
            g1 = kb.op(
                "scalar",
                lambda e, i=pb, o=GB[:, b2i * 512 : b2i * 512 + 512],
                b=badd[:, oc : oc + 1]: e.activation(o, i, AF.Gelu, bias=b, scale=1.0),
                "act",
            )
            gd = kb.dma(
                "sync", gs[oc * 128 : (oc + 1) * 128, tt * 512 : tt * 512 + 512],
                GB[:, b2i * 512 : b2i * 512 + 512], deps=g1, sem=f"hs{b2i}",
            )
            prev_gdma[b2i] = gd
            prev_psum7[bi] = g1
            g_store.append(gd)
        prev_xq = kb.now("pe")

    mm1_done = kb.now("pe")
    v_deps = []
    prev_v = None
    for hc in range(HC):
        kb.wait("sync", mm1_done)
        d = kb.dma(
            "sync", AW[:, hc * 1024 : (hc + 1) * 1024],
            vscr[hc * 128 : (hc + 1) * 128, :],
            deps=[kb.now("sv0"), kb.now("sv1")], sem=f"lw{hc % 4}",
        )
        v_deps.append(d)

    bcc = v.small[:, 200:208]
    dbc = v.small[:, 208:216]
    bsum = v.small[:, 224:232]
    k1 = kb.dma("sync", bcc, dram["l1_bcc"], sem="lb3")
    k2 = kb.dma("sync", dbc, dram["l1_dbc"], sem="lb3")
    kb.wait("vector", [k1, k2])
    k3 = kb.op("vector", lambda e: e.tensor_add(bsum, bcc, dbc), "dve")
    res_deps = []
    prev_ep = [prev_psum7[0], prev_psum7[1], prev_psum7[2], prev_psum7[3]]
    gl_prev = [kb.now("hs0"), kb.now("hs1")]
    for tt in range(NT):
        gl_deps = []
        for hc in range(HC):
            d = kb.dma(
                "sync", GL[:, hc * 512 : (hc + 1) * 512],
                gs[hc * 128 : (hc + 1) * 128, tt * 512 : tt * 512 + 512],
                deps=gl_prev, sem=f"hr{tt % 2}",
            )
            gl_deps.append(d)
        for dc in range(DC):
            bi = dc % 4
            pb = bank[bi]
            kb.wait("tensor", v_deps)
            kb.wait("tensor", gl_deps)
            kb.wait("tensor", prev_ep[bi])
            mm = None
            for hc in range(HC):
                mm = kb.op(
                    "tensor",
                    lambda e, o=pb, l=AW[:, hc * 1024 + dc * 128 : hc * 1024 + dc * 128 + 128],
                    r=GL[:, hc * 512 : (hc + 1) * 512],
                    s=(hc == 0), t=(hc == HC - 1): e.matmul(o, l, r, start=s, stop=t),
                    "pe",
                )
            sl = stream_tile(v, dc, tt)
            kb.wait("vector", mm)
            kb.wait("vector", [x_dep, k3])
            ep = kb.op(
                "vector",
                lambda e, s=sl, i=pb, b=bsum[:, dc : dc + 1]: e.scalar_tensor_tensor(
                    s, i, b, s, ALU.add, ALU.add
                ),
                "dve",
            )
            prev_ep[bi] = ep
            res_deps.append(ep)
        gl_prev = kb.now("pe")

    kb.barrier()
    return emit_ln(kb, v, ps, res_deps)


def host_consts():
    c = np.zeros((128, 260), np.float32)
    r = np.arange(128)
    c[:, 0:128] = (r[:, None] <= np.arange(128)[None, :]).astype(np.float32)
    c[:, 128:256] = np.eye(128, dtype=np.float32)
    return c


def host_prep(inputs, blocks):
    m = {"consts": host_consts()}
    if 0 in blocks:
        m["l0_wkT"] = np.ascontiguousarray(
            inputs["L0_attn_W_K"].reshape(1024, 1024).T
        )
        m["l0_wqT"] = np.ascontiguousarray(
            inputs["L0_attn_W_Q"].reshape(1024, 1024).T
        )
        m["l0_wvT"] = np.ascontiguousarray(
            inputs["L0_attn_W_V"].reshape(1024, 1024).T
        )
        m["l0_woT"] = np.ascontiguousarray(inputs["L0_attn_W_O"].T)
    if 1 in blocks:
        m["l0_w1T"] = np.ascontiguousarray(inputs["L0_ffn_w1"].T)
        m["l0_w2T"] = np.ascontiguousarray(inputs["L0_ffn_w2"].T)
        m["l0_b1c"] = np.ascontiguousarray(inputs["L0_ffn_b1"].reshape(32, 128).T)
        m["l0_b2c"] = np.ascontiguousarray(inputs["L0_ffn_b2"].reshape(8, 128).T)
    if 2 in blocks:
        m["l1_waT"] = np.ascontiguousarray(inputs["L1_attn_W_A"].T)
        m["l1_wvT"] = np.ascontiguousarray(inputs["L1_attn_W_V"].T)
        m["l1_woT"] = np.ascontiguousarray(inputs["L1_attn_W_O"].T)
        m["l1_bOc"] = np.ascontiguousarray(inputs["L1_attn_b_O"].reshape(8, 128).T)
    if 3 in blocks:
        m.update(host_prep_l1ffn(inputs))
    return m


def _scatter_stage(wstage, s, transpose):
    sig = 1 << s
    A = np.zeros((4096, 4096), np.float32)
    k = np.arange(2048)
    i = (k // sig) * 2 * sig + (k % sig)
    A[i, i] = wstage[:, 0, 0]
    A[i, i + sig] = wstage[:, 0, 1]
    A[i + sig, i] = wstage[:, 1, 0]
    A[i + sig, i + sig] = wstage[:, 1, 1]
    if transpose:
        A = A.T
    out = np.zeros((32, 128, 128), np.float32)
    for j in range(32):
        out[j] = A[j * 128 : (j + 1) * 128, j * 128 : (j + 1) * 128]
    return out


def _perm_P():
    f = np.arange(4096)
    c = f // 128
    p = f % 128
    fp = p * 32 + c
    return fp


def _scatter_stage_relab(wstage, s):
    sig = 1 << s
    A = np.zeros((4096, 4096), np.float32)
    k = np.arange(2048)
    i = (k // sig) * 2 * sig + (k % sig)
    A[i, i] = wstage[:, 0, 0]
    A[i, i + sig] = wstage[:, 0, 1]
    A[i + sig, i] = wstage[:, 1, 0]
    A[i + sig, i + sig] = wstage[:, 1, 1]
    fp = _perm_P()
    Ar = np.zeros_like(A)
    Ar[np.ix_(fp, fp)] = A
    return Ar


def _blocks_of(A):
    out = np.zeros((32, 128, 128), np.float32)
    for j in range(32):
        out[j] = A[j * 128 : (j + 1) * 128, j * 128 : (j + 1) * 128]
    return out


def _chain_stack(mats_blocks):
    k = len(mats_blocks)
    out = [np.transpose(mats_blocks[-1], (0, 2, 1))]
    for i in range(k - 2, -1, -1):
        out.append(mats_blocks[i])
    return np.stack(out)


def _pack_stk(stk):
    return np.ascontiguousarray(np.transpose(stk, (1, 2, 0, 3)).reshape(32, 128, 12 * 128))


def host_prep_l1ffn(inputs):
    m = {}
    wexp = inputs["L1_ffn_W_exp"]
    bexp = inputs["L1_ffn_b_exp"]
    m["l1_wexpA"] = np.ascontiguousarray(
        np.concatenate([np.asarray(wexp).T, np.asarray(bexp)[:, None]], axis=1)
    )
    fp = _perm_P()
    wcon = np.asarray(inputs["L1_ffn_W_con"])
    wconP = np.zeros_like(wcon)
    wconP[fp] = wcon
    m["l1_wconP"] = np.ascontiguousarray(wconP)

    upw = np.asarray(inputs["L1_ffn_up_weights"])
    dnw = np.asarray(inputs["L1_ffn_down_weights"])

    up_g1 = [_scatter_stage(upw[s], s, transpose=False) for s in range(7)]
    up_g2 = [_blocks_of(_scatter_stage_relab(upw[s], s)) for s in range(7, 12)]
    st1 = _chain_stack(up_g1)
    st2 = _chain_stack(up_g2)
    m["l1_upstk"] = _pack_stk(np.concatenate([st1, st2], axis=0))

    dn_g2 = [
        np.transpose(_blocks_of(_scatter_stage_relab(dnw[s], s)), (0, 2, 1))
        for s in range(11, 6, -1)
    ]
    dn_g1 = [
        _scatter_stage(dnw[s], s, transpose=True) for s in range(6, -1, -1)
    ]
    st1d = _chain_stack(dn_g2)
    st2d = _chain_stack(dn_g1)
    m["l1_dnstk"] = _pack_stk(np.concatenate([st1d, st2d], axis=0))

    m["l1_ubc"] = np.ascontiguousarray(
        np.asarray(inputs["L1_ffn_up_bias"]).reshape(32, 128).T
    )
    bsum_src = np.stack(
        [np.asarray(inputs["L1_ffn_b_con"]), np.asarray(inputs["L1_ffn_down_bias"])]
    )
    m["l1_bcc"] = np.ascontiguousarray(
        bsum_src.reshape(2, 8, 128).transpose(2, 0, 1).reshape(128, 16)[:, 0:8]
    )
    m["l1_bcc"] = np.ascontiguousarray(
        np.asarray(inputs["L1_ffn_b_con"]).reshape(8, 128).T
    )
    m["l1_dbc"] = np.ascontiguousarray(
        np.asarray(inputs["L1_ffn_down_bias"]).reshape(8, 128).T
    )
    return m


_NC_CACHE = {}


def kernel(**inputs):
    x = np.asarray(inputs["L0_attn_x"])
    ncores = 8
    blocks_all = [b for launch in LAUNCH_PLAN for b in launch]
    common = host_prep(inputs, blocks_all)

    xs = [np.ascontiguousarray(x[b].T) for b in range(ncores)]
    cur = xs
    for launch in LAUNCH_PLAN:
        key = tuple(launch)
        if key not in _NC_CACHE:
            _NC_CACHE[key] = build_program(list(launch))
        nc, dram = _NC_CACHE[key]
        in_maps = []
        for b in range(ncores):
            im = {"xin": cur[b]}
            for k, arr in common.items():
                if k in dram:
                    im[k] = arr
            in_maps.append(im)
        res = run_bass_kernel_spmd(nc, in_maps, core_ids=list(range(ncores)))
        cur = [np.asarray(res.results[b]["xout"]) for b in range(ncores)]

    out = np.stack([c.T for c in cur])
    return np.ascontiguousarray(out.astype(np.float32))


# version 13
# speedup vs baseline: 1.0058x; 1.0058x over previous
import numpy as np
import ml_dtypes
import concourse.bass as bass
import concourse.mybir as mybir
from concourse.bass_utils import run_bass_kernel_spmd

F32 = mybir.dt.float32
BF16 = mybir.dt.bfloat16
AF = mybir.ActivationFunctionType
ALU = mybir.AluOpType
AX = mybir.AxisListType

MAGIC = 12582912.0
D = 1024
S = 2048
NH = 16
DHEAD = 64
HID = 4096
DC = D // 128
HC = HID // 128
NT = S // 512
EPS = 1e-5

LAUNCH_PLAN = [[0, 1, 2, 3]]

BF = ml_dtypes.bfloat16


class KB:
    ENGS = ("sync", "gpsimd", "tensor", "vector", "scalar")
    SEMS = ("pe", "dve", "act")

    def __init__(self, nc):
        self.nc = nc
        self.sched = {e: [] for e in self.ENGS}
        self.cnt = {s: 0 for s in self.SEMS}
        self.last_wait = {}
        self.sem_h = None

    def _flat(self, dep, out):
        if dep is None:
            return
        if isinstance(dep, (list, tuple)) and dep and isinstance(dep[0], (list, tuple, type(None))):
            for d in dep:
                self._flat(d, out)
            return
        out.append(dep)

    def wait(self, eng, dep):
        flat = []
        self._flat(dep, flat)
        by_sem = {}
        for sem, val in flat:
            if val > by_sem.get(sem, 0):
                by_sem[sem] = val
        for sem, val in by_sem.items():
            k = (eng, sem)
            if self.last_wait.get(k, 0) >= val:
                continue
            self.last_wait[k] = val
            self.sched[eng].append(lambda e, s=sem, v=val: e.wait_ge(self.sem_h[s], v))

    def op(self, eng, fn, sem=None, amt=1):
        if eng in ("vector", "scalar"):
            own = "dve" if eng == "vector" else "act"
            self.wait(eng, (own, self.cnt.get(own, 0)))
        if sem is None:
            self.sched[eng].append(fn)
            return None
        if sem not in self.cnt:
            self.cnt[sem] = 0
        self.cnt[sem] += amt
        v = self.cnt[sem]
        self.sched[eng].append(
            lambda e, f=fn, s=sem, a=amt: f(e).then_inc(self.sem_h[s], a)
        )
        return (sem, v)

    def dma(self, queue, out, in_, deps=None, sem=None, **kw):
        self.wait(queue, deps)
        if sem is None:
            sem = "ld" if queue == "sync" else "gp"
        if sem not in self.cnt:
            self.cnt[sem] = 0
        return self.op(queue, lambda e: e.dma_start(out=out, in_=in_, **kw), sem, 16)

    def now(self, sem):
        return (sem, self.cnt.get(sem, 0))

    def barrier(self):
        snap = [(s, v) for s, v in self.cnt.items() if v > 0]
        for e in self.ENGS:
            for d in snap:
                self.wait(e, d)


class Carve:
    def __init__(self, nc, total_bytes=206 * 1024):
        self.total = total_bytes
        self.mega = nc.alloc_sbuf_tensor("mega", [128, total_bytes // 2], BF16)
        self.mega_f32 = self.mega.bitcast(F32).reshape([128, total_bytes // 4])
        self.perm_off = 0

    def bf(self, byte_off, cols):
        assert byte_off % 32 == 0 and byte_off + 2 * cols <= self.total
        return self.mega[:, byte_off // 2 : byte_off // 2 + cols]

    def f32(self, byte_off, cols):
        assert byte_off % 32 == 0 and byte_off + 4 * cols <= self.total
        return self.mega_f32[:, byte_off // 4 : byte_off // 4 + cols]


OFF_STREAM = 0
OFF_CONST = 65536
OFF_ROWS = OFF_CONST + 2048
OFF_SMALL = OFF_ROWS + 8192
OFF_PHASE = OFF_SMALL + 2048
PHASE_BYTES = 206 * 1024 - OFF_PHASE


class Views:

    def __init__(self, cv: Carve):
        self.cv = cv
        self.stream = cv.f32(OFF_STREAM, 16384)
        self.tril = cv.bf(OFF_CONST, 128)
        self.id128 = cv.bf(OFF_CONST + 256, 128)
        self.ones_bf = cv.bf(OFF_CONST + 512, 1)
        self.magic = cv.f32(OFF_CONST + 544, 1)
        self.epsc = cv.f32(OFF_CONST + 576, 1)
        self.ones1x64 = cv.f32(OFF_CONST + 608, 64)
        self.ones1x128 = cv.f32(OFF_CONST + 864, 128)
        self.rows = cv.f32(OFF_ROWS, 2048)
        self.small = cv.f32(OFF_SMALL, 512)

    def row(self, i, width=512):
        return self.rows[0:1, i * 512 : i * 512 + width]

    def stream_sl(self, dc, c0, c1):
        return self.stream[:, dc * 2048 + c0 : dc * 2048 + c1]


def stream_tile(v, dc, qt):
    return v.stream[:, dc * 2048 + qt * 512 : dc * 2048 + (qt + 1) * 512]


def emit_ln(kb: KB, v: Views, ps, r_dep):
    nc = kb.nc
    psBig = ps["big"]
    rbf = ps["lnbuf0"]
    sqbf = ps["lnbuf1"]
    tmpf = ps["lntmp"]
    for qt in range(NT):
        stat_deps = []
        prev_cast = None
        for dc in range(DC):
            sl = stream_tile(v, dc, qt)
            kb.wait("vector", r_dep)
            kb.wait("vector", prev_cast)
            c1 = kb.op("vector", lambda e, o=rbf, i=sl: e.tensor_copy(o, i), "dve")
            kb.wait("scalar", r_dep)
            kb.wait("scalar", prev_cast)
            c2 = kb.op(
                "scalar",
                lambda e, o=sqbf, i=sl: e.activation(o, i, AF.Square),
                "act",
            )
            kb.wait("tensor", c1)
            m1 = kb.op(
                "tensor",
                lambda e, o=psBig[0:1, 0:512], l=v.ones_bf, r=rbf, s=(dc == 0), t=(
                    dc == DC - 1
                ): e.matmul(o, l, r, start=s, stop=t),
                "pe",
            )
            kb.wait("tensor", c2)
            m2 = kb.op(
                "tensor",
                lambda e, o=psBig[0:1, 512:1024], l=v.ones_bf, r=sqbf, s=(dc == 0), t=(
                    dc == DC - 1
                ): e.matmul(o, l, r, start=s, stop=t),
                "pe",
            )
            prev_cast = [m1, m2]
            stat_deps = [m1, m2]
        r_mean = v.row(0)
        r_msq = v.row(1)
        r_rstd = v.row(2)
        r_bneg = v.row(3)
        kb.wait("vector", stat_deps)
        d1 = kb.op(
            "vector",
            lambda e: e.tensor_scalar_mul(r_mean, psBig[0:1, 0:512], 1.0 / D),
            "dve",
        )
        d2 = kb.op(
            "vector",
            lambda e: e.tensor_scalar_mul(r_msq, psBig[0:1, 512:1024], 1.0 / D),
            "dve",
        )
        d3 = kb.op(
            "vector", lambda e: e.tensor_mul(r_bneg, r_mean, r_mean), "dve"
        )
        d4 = kb.op(
            "vector", lambda e: e.tensor_sub(r_msq, r_msq, r_bneg), "dve"
        )
        kb.wait("scalar", d4)
        d5 = kb.op(
            "scalar",
            lambda e: e.activation(
                r_rstd, r_msq, AF.Sqrt, bias=v.epsc[0:1, 0:1], scale=1.0
            ),
            "act",
        )
        kb.wait("vector", d5)
        d6 = kb.op("vector", lambda e: e.reciprocal(r_rstd, r_rstd), "dve")
        kb.wait("vector", d6)
        d7 = kb.op("vector", lambda e: e.tensor_mul(r_bneg, r_mean, r_rstd), "dve")
        d8 = kb.op(
            "vector", lambda e: e.tensor_scalar_mul(r_bneg, r_bneg, -1.0), "dve"
        )
        kb.wait("tensor", [d6, d8, d1, d2])
        b1 = kb.op(
            "tensor",
            lambda e: e.matmul(
                psBig[:, 0:512], v.ones1x128[0:1, :], r_rstd, start=True, stop=True
            ),
            "pe",
        )
        b2 = kb.op(
            "tensor",
            lambda e: e.matmul(
                psBig[:, 512:1024], v.ones1x128[0:1, :], r_bneg, start=True, stop=True
            ),
            "pe",
        )
        app_deps = []
        for dc in range(DC):
            sl = stream_tile(v, dc, qt)
            kb.wait("vector", [b1, b2])
            a1 = kb.op(
                "vector",
                lambda e, s=sl: e.tensor_mul(tmpf, s, psBig[:, 0:512]),
                "dve",
            )
            kb.wait("vector", a1)
            a2 = kb.op(
                "vector",
                lambda e, s=sl: e.tensor_add(s, tmpf, psBig[:, 512:1024]),
                "dve",
            )
            app_deps.append(a2)
        kb.wait("tensor", app_deps)
    return kb.now("dve")


def emit_l0_attn(kb: KB, v: Views, dram, ps, x_dep):
    nc = kb.nc
    cv = v.cv
    A = OFF_PHASE
    xbf = cv.bf(A, 16384)
    vtm = cv.bf(A + 32768, 16640)
    zfm = cv.bf(A + 66048, 16384)
    kqb = cv.bf(A + 98816, 4096)
    ws = cv.bf(A + 107008, 8192)
    wkq = cv.bf(A + 123392, 2048)
    exb = cv.bf(A + 127488, 2048)
    ps["lnbuf0"] = cv.bf(A + 98816, 512)
    ps["lnbuf1"] = cv.bf(A + 99840, 512)
    ps["lntmp"] = cv.f32(A + 100864, 512)

    pAB, pCD = ps["pAB"], ps["pCD"]
    p5 = ps["p5"]
    psBig = ps["big"]
    b4 = ps["b4"]

    xbf_deps = []
    for dc in range(DC):
        d = kb.dma(
            "gpsimd", xbf[:, dc * 2048 : (dc + 1) * 2048],
            dram["xin"][dc * 128 : (dc + 1) * 128, :], sem="ga",
        )
        xbf_deps.append(d)
    wv_deps = []
    for dc in range(DC):
        d = kb.dma(
            "gpsimd", ws[:, dc * 1024 : (dc + 1) * 1024],
            dram["l0_wvT"][dc * 128 : (dc + 1) * 128, :], sem="gb",
        )
        wv_deps.append(d)

    vtm4 = vtm.rearrange("p (a h e) -> p a h e", a=16, h=16)
    ones_dep = kb.op("vector", lambda e: e.memset(vtm4[:, :, :, 64:65], 1.0), "dve")

    vtm_deps = [ones_dep]
    prev_evac = [None, None, None, None]
    for pt in range(16):
        for fh in range(2):
            bi = (pt * 2 + fh) % 4
            pb = b4[bi]
            kb.wait("tensor", xbf_deps)
            kb.wait("tensor", wv_deps)
            kb.wait("tensor", prev_evac[bi])
            mm = None
            for dc in range(DC):
                mm = kb.op(
                    "tensor",
                    lambda e, o=pb, l=xbf[:, dc * 2048 + pt * 128 : dc * 2048 + pt * 128 + 128],
                    r=ws[:, dc * 1024 + fh * 512 : dc * 1024 + fh * 512 + 512],
                    s=(dc == 0), t=(dc == DC - 1): e.matmul(o, l, r, start=s, stop=t),
                    "pe",
                )
            out_ap = vtm4[:, pt, fh * 8 : (fh + 1) * 8, 0:64]
            kb.wait("vector", mm)
            ev = kb.op(
                "vector",
                lambda e, o=out_ap, i=pb: e.tensor_copy(
                    o, i.rearrange("p (h e) -> p h e", h=8)
                ),
                "dve",
            )
            prev_evac[bi] = ev
            vtm_deps.append(ev)

    zfm_deps = []
    last_kq_use = None
    pend_n3 = [None, None]
    pend_n4 = [None]
    prev_n5 = [None, None]
    zbanks = [p5[0:65, 0:512], psBig[0:65, 1024:1536]]
    zpar = 0
    pend = []

    def flush_pend(all_=False):
        want = 0
        while len(pend) > want:
            pend.pop(0)()

    ex_guard = [None, None]
    sp_guard = [None, None]

    for hp in range(8):
        wk_d = []
        for dc in range(DC):
            d = kb.dma(
                "gpsimd", wkq[:, dc * 128 : dc * 128 + 128],
                dram["l0_wkT"][dc * 128 : (dc + 1) * 128, hp * 128 : hp * 128 + 128],
                deps=last_kq_use, sem="gw",
            )
            wk_d.append(d)
            d2 = kb.dma(
                "gpsimd", wkq[:, 1024 + dc * 128 : 1024 + dc * 128 + 128],
                dram["l0_wqT"][dc * 128 : (dc + 1) * 128, hp * 128 : hp * 128 + 128],
                sem="gw",
            )
            wk_d.append(d2)
        kq_deps = []
        for which in range(2):
            for tt in range(NT):
                bi = (which * NT + tt) % 4
                pb = b4[bi]
                kb.wait("tensor", wk_d)
                kb.wait("tensor", xbf_deps)
                kb.wait("tensor", prev_evac[bi])
                mm = None
                for dc in range(DC):
                    mm = kb.op(
                        "tensor",
                        lambda e, o=pb,
                        l=wkq[:, which * 1024 + dc * 128 : which * 1024 + dc * 128 + 128],
                        r=xbf[:, dc * 2048 + tt * 512 : dc * 2048 + tt * 512 + 512],
                        s=(dc == 0), t=(dc == DC - 1): e.matmul(o, l, r, start=s, stop=t),
                        "pe",
                    )
                out_ap = kqb[:, which * 2048 + tt * 512 : which * 2048 + tt * 512 + 512]
                kb.wait("scalar", mm)
                ev = kb.op(
                    "scalar",
                    lambda e, o=out_ap, i=pb: e.activation(o, i, AF.Identity),
                    "act",
                )
                prev_evac[bi] = ev
                kq_deps.append(ev)

        last_kq_use = kb.now("pe")

        for qt in range(NT):
            nch = 4 * qt + 4
            zmm = [None, None]
            for pc2 in range(nch // 2):
                c0, c1 = 2 * pc2, 2 * pc2 + 1
                rel0 = 128 * c0 - 512 * qt
                rel1 = 128 * c1 - 512 * qt
                n00 = max(rel0, 0)
                n01 = max(rel1, 0)
                for hl in range(2):
                    head = 2 * hp + hl
                    hr0 = 64 * hl
                    pz = zbanks[hl]
                    spb = [pAB, pCD][hl]
                    eb = exb[:, hl * 1024 : hl * 1024 + 1024]
                    kb.wait("tensor", kq_deps)
                    kb.wait("tensor", sp_guard[hl])
                    kb.wait("tensor", prev_evac[2 * hl])
                    kb.wait("tensor", prev_evac[2 * hl + 1])
                    sm0 = kb.op(
                        "tensor",
                        lambda e, o=spb[:, n00:512],
                        l=kqb[hr0 : hr0 + 64, c0 * 128 : c0 * 128 + 128],
                        r=kqb[hr0 : hr0 + 64, 2048 + qt * 512 + n00 : 2048 + qt * 512 + 512]:
                        e.matmul(o, l, r, start=True, stop=True),
                        "pe",
                    )
                    sm1 = kb.op(
                        "tensor",
                        lambda e, o=spb[:, 512 + n01 : 1024],
                        l=kqb[hr0 : hr0 + 64, c1 * 128 : c1 * 128 + 128],
                        r=kqb[hr0 : hr0 + 64, 2048 + qt * 512 + n01 : 2048 + qt * 512 + 512]:
                        e.matmul(o, l, r, start=True, stop=True),
                        "pe",
                    )
                    if pc2 == 0 and hl == 0:
                        flush_pend()
                    kb.wait("scalar", sm1)
                    kb.wait("scalar", ex_guard[hl])
                    if rel0 >= 0:
                        e0 = kb.op(
                            "scalar",
                            lambda e, o=eb[:, n00:512], i=spb[:, n00:512]:
                            e.activation(o, i, AF.Exp, bias=0.0, scale=0.125),
                            "act",
                        )
                        e1 = kb.op(
                            "scalar",
                            lambda e, o=eb[:, 512 + n01 : 1024], i=spb[:, 512 + n01 : 1024]:
                            e.activation(o, i, AF.Exp, bias=0.0, scale=0.125),
                            "act",
                        )
                        kb.wait("vector", [e0, e1])
                        t0 = kb.op(
                            "vector",
                            lambda e, o=eb[:, rel0 : rel0 + 128]: e.tensor_mul(
                                o, o, v.tril
                            ),
                            "dve",
                        )
                        t1 = kb.op(
                            "vector",
                            lambda e, o=eb[:, 512 + rel1 : 512 + rel1 + 128]: e.tensor_mul(
                                o, o, v.tril
                            ),
                            "dve",
                        )
                        edep = [t0, t1]
                        sp_guard[hl] = [e0, e1]
                    else:
                        ef = kb.op(
                            "scalar",
                            lambda e, o=eb, i=spb[:]: e.activation(
                                o, i, AF.Exp, bias=0.0, scale=0.125
                            ),
                            "act",
                        )
                        edep = ef
                        sp_guard[hl] = ef
                    kb.wait("tensor", edep)
                    kb.wait("tensor", vtm_deps)
                    if pc2 == 0:
                        kb.wait("tensor", prev_n5[hl])
                    z0 = kb.op(
                        "tensor",
                        lambda e, o=pz[:, n00:512],
                        l=vtm[:, c0 * 1040 + head * 65 : c0 * 1040 + head * 65 + 65],
                        r=eb[:, n00:512], s=(c0 == 0), t=False:
                        e.matmul(o, l, r, start=s, stop=t),
                        "pe",
                    )
                    z1 = kb.op(
                        "tensor",
                        lambda e, o=pz[:, n01:512],
                        l=vtm[:, c1 * 1040 + head * 65 : c1 * 1040 + head * 65 + 65],
                        r=eb[:, 512 + n01 : 1024], s=False, t=(c1 == nch - 1):
                        e.matmul(o, l, r, start=s, stop=t),
                        "pe",
                    )
                    zmm[hl] = z1
                    ex_guard[hl] = z1
            for hl in range(2):
                head = 2 * hp + hl
                hr0 = 64 * hl
                pz = zbanks[hl]
                zr = v.row(hl)
                kb.wait("vector", zmm[hl])
                kb.wait("vector", pend_n3[hl])
                n1 = kb.op(
                    "vector", lambda e, i=pz, z=zr: e.tensor_copy(z, i[64:65, :]), "dve"
                )
                kb.wait("vector", n1)
                n2 = kb.op("vector", lambda e, z=zr: e.reciprocal(z, z), "dve")
                out_ap = zfm[hr0 : hr0 + 64,
                             hp * 2048 + qt * 512 : hp * 2048 + qt * 512 + 512]

                def mk_norm(n2=n2, pz=pz, out_ap=out_ap, zp=hl, zr=zr):
                    def emitit():
                        kb.wait("tensor", n2)
                        kb.wait("tensor", pend_n4[0])
                        n3 = kb.op(
                            "tensor",
                            lambda e, z=zr: e.matmul(
                                psBig[0:64, 0:512], v.ones1x64[0:1, :], z,
                                start=True, stop=True,
                            ),
                            "pe",
                        )
                        kb.wait("vector", n3)
                        n4 = kb.op(
                            "vector",
                            lambda e: e.tensor_copy(
                                v.small[0:64, 0:512], psBig[0:64, 0:512]
                            ),
                            "dve",
                        )
                        kb.wait("vector", n4)
                        n5 = kb.op(
                            "vector",
                            lambda e, o=out_ap, i=pz: e.tensor_mul(
                                o, i[0:64, :], v.small[0:64, 0:512]
                            ),
                            "dve",
                        )
                        zfm_deps.append(n5)
                        pend_n3[zp] = n3
                        pend_n4[0] = n4
                        prev_n5[zp] = n5
                    return emitit

                pend.append(mk_norm())

    flush_pend(all_=True)

    wo_deps = []
    kb.wait("gpsimd", kb.now("pe"))
    for fc in range(DC):
        d = kb.dma(
            "gpsimd", ws[:, fc * 1024 : (fc + 1) * 1024],
            dram["l0_woT"][fc * 128 : (fc + 1) * 128, :], sem="gb",
        )
        wo_deps.append(d)
    res_deps = []
    for dc in range(DC):
        for qt in range(NT):
            bi = (dc * NT + qt) % 4
            pb = b4[bi]
            kb.wait("tensor", wo_deps)
            kb.wait("tensor", zfm_deps)
            kb.wait("tensor", prev_evac[bi])
            kb.wait("tensor", sp_guard[bi // 2])
            mm = None
            for fc in range(DC):
                mm = kb.op(
                    "tensor",
                    lambda e, o=pb,
                    l=ws[:, fc * 1024 + dc * 128 : fc * 1024 + dc * 128 + 128],
                    r=zfm[:, fc * 2048 + qt * 512 : fc * 2048 + qt * 512 + 512],
                    s=(fc == 0), t=(fc == DC - 1): e.matmul(o, l, r, start=s, stop=t),
                    "pe",
                )
            sl = stream_tile(v, dc, qt)
            kb.wait("vector", mm)
            kb.wait("vector", x_dep)
            ev = kb.op(
                "vector", lambda e, s=sl, i=pb: e.tensor_add(s, i, s), "dve"
            )
            prev_evac[bi] = ev
            res_deps.append(ev)

    kb.barrier()
    return emit_ln(kb, v, ps, res_deps)


def declare_dram(nc, blocks):
    dram = {}
    dram["xin"] = nc.dram_tensor("xin", [D, S], F32, kind="ExternalInput").ap()
    dram["xout"] = nc.dram_tensor("xout", [D, S], F32, kind="ExternalOutput").ap()
    dram["consts"] = nc.dram_tensor("consts", [128, 260], F32, kind="ExternalInput").ap()
    if 0 in blocks:
        dram["l0_wkT"] = nc.dram_tensor("l0_wkT", [D, D], F32, kind="ExternalInput").ap()
        dram["l0_wqT"] = nc.dram_tensor("l0_wqT", [D, D], F32, kind="ExternalInput").ap()
        dram["l0_wvT"] = nc.dram_tensor("l0_wvT", [D, D], F32, kind="ExternalInput").ap()
        dram["l0_woT"] = nc.dram_tensor("l0_woT", [D, D], F32, kind="ExternalInput").ap()
    if 1 in blocks:
        dram["l0_w1T"] = nc.dram_tensor("l0_w1T", [D, HID], F32, kind="ExternalInput").ap()
        dram["l0_w2T"] = nc.dram_tensor("l0_w2T", [HID, D], F32, kind="ExternalInput").ap()
        dram["l0_b1c"] = nc.dram_tensor("l0_b1c", [128, 32], F32, kind="ExternalInput").ap()
        dram["l0_b2c"] = nc.dram_tensor("l0_b2c", [128, 8], F32, kind="ExternalInput").ap()
        import os as _os
        _hk = "ExternalOutput" if _os.environ.get("KDBG_H") else "Internal"
        dram["hscratch"] = nc.dram_tensor("hscratch", [HID, S], BF16, kind=_hk).ap()
        if _os.environ.get("KDBG_H"):
            dram["dbg_xq"] = nc.dram_tensor("dbg_xq", [128, 4096], BF16, kind="ExternalOutput").ap()
            dram["dbg_sm"] = nc.dram_tensor("dbg_sm", [128, 512], F32, kind="ExternalOutput").ap()
            dram["dbg_w2"] = nc.dram_tensor("dbg_w2", [128, 32768], BF16, kind="ExternalOutput").ap()
            dram["dbg_r"] = nc.dram_tensor("dbg_r", [D, S], F32, kind="ExternalOutput").ap()
            dram["dbg_w1"] = nc.dram_tensor("dbg_w1", [128, 32768], BF16, kind="ExternalOutput").ap()

    if 2 in blocks:
        dram["l1_waT"] = nc.dram_tensor("l1_waT", [D, NH], F32, kind="ExternalInput").ap()
        dram["l1_wvT"] = nc.dram_tensor("l1_wvT", [D, D], F32, kind="ExternalInput").ap()
        dram["l1_woT"] = nc.dram_tensor("l1_woT", [D, D], F32, kind="ExternalInput").ap()
        dram["l1_bOc"] = nc.dram_tensor("l1_bOc", [128, 8], F32, kind="ExternalInput").ap()
        dram["wrev"] = nc.dram_tensor("wrev", [NH, 2560], BF16, kind="Internal").ap()
    if 3 in blocks:
        dram["l1_wexpA"] = nc.dram_tensor("l1_wexpA", [HID, D + 1], F32, kind="ExternalInput").ap()
        dram["l1_wconP"] = nc.dram_tensor("l1_wconP", [HID, D], F32, kind="ExternalInput").ap()
        dram["l1_upstk"] = nc.dram_tensor("l1_upstk", [32, 128, 1536], F32, kind="ExternalInput").ap()
        dram["l1_dnstk"] = nc.dram_tensor("l1_dnstk", [32, 128, 1536], F32, kind="ExternalInput").ap()
        dram["l1_ubc"] = nc.dram_tensor("l1_ubc", [128, 32], F32, kind="ExternalInput").ap()
        dram["l1_bcc"] = nc.dram_tensor("l1_bcc", [128, 8], F32, kind="ExternalInput").ap()
        dram["l1_dbc"] = nc.dram_tensor("l1_dbc", [128, 8], F32, kind="ExternalInput").ap()
        dram["cscr1"] = nc.dram_tensor("cscr1", [HID, D + 2], BF16, kind="Internal").ap()
        dram["cscr2"] = nc.dram_tensor("cscr2", [HID, D + 128], BF16, kind="Internal").ap()
        dram["vscr"] = nc.dram_tensor("vscr", [HID, D], BF16, kind="Internal").ap()
        dram["bscr"] = nc.dram_tensor("bscr", [32, 128], BF16, kind="Internal").ap()
        dram["gscratch"] = nc.dram_tensor("gscratch", [HID, S], BF16, kind="Internal").ap()
    return dram


def build_program(blocks):
    nc = bass.Bass("TRN2", target_bir_lowering=False)
    dram = declare_dram(nc, blocks)
    cv = Carve(nc)
    v = Views(cv)
    kb = KB(nc)

    import contextlib

    stack = contextlib.ExitStack()
    psd = {}
    psd["big"] = stack.enter_context(nc.psum_tensor("psBig", [128, 1536], F32))
    pAB = stack.enter_context(nc.psum_tensor("pAB", [128, 1024], F32))
    pCD = stack.enter_context(nc.psum_tensor("pCD", [128, 1024], F32))
    p5h = stack.enter_context(nc.psum_tensor("p5", [128, 512], F32))
    psd["pAB"] = pAB
    psd["pCD"] = pCD
    psd["p3"] = pAB[:, 0:512]
    psd["p4"] = pAB[:, 512:1024]
    psd["p5"] = p5h[:]
    psd["p6"] = psd["big"][:, 1024:1536]
    psd["b4"] = [pAB[:, 0:512], pAB[:, 512:1024], pCD[:, 0:512], pCD[:, 512:1024]]
    psd["pT"] = p5h.bitcast(BF16).reshape([128, 1024])[0:32, 0:128]


    cdep = []
    cdep.append(kb.dma("gpsimd", v.tril, dram["consts"][:, 0:128], sem="gc"))
    cdep.append(kb.dma("gpsimd", v.id128, dram["consts"][:, 128:256], sem="gc"))
    mset = [
        kb.op("vector", lambda e: e.memset(v.small, 0.0), "dve"),
        kb.op("vector", lambda e: e.memset(v.ones_bf, 1.0), "dve"),
        kb.op("vector", lambda e: e.memset(v.magic, MAGIC), "dve"),
        kb.op("vector", lambda e: e.memset(v.epsc, EPS), "dve"),
        kb.op("vector", lambda e: e.memset(v.ones1x64[0:1, :], 1.0), "dve"),
        kb.op("vector", lambda e: e.memset(v.ones1x128[0:1, :], 1.0), "dve"),
    ]
    kb.barrier()

    x_dep = []
    for dc in range(DC):
        d = kb.dma(
            "sync", v.stream[:, dc * 2048 : (dc + 1) * 2048],
            dram["xin"][dc * 128 : (dc + 1) * 128, :], sem="lx",
        )
        x_dep.append(d)

    dep = x_dep
    for b in blocks:
        if b == 0:
            dep = emit_l0_attn(kb, v, dram, psd, dep)
        elif b == 1:
            dep = emit_l0_ffn(kb, v, dram, psd, dep)
        elif b == 2:
            dep = emit_l1_attn(kb, v, dram, psd, dep)
        elif b == 3:
            dep = emit_l1_ffn(kb, v, dram, psd, dep)
        kb.barrier()

    st = []
    for dc in range(DC):
        d = kb.dma(
            "sync", dram["xout"][dc * 128 : (dc + 1) * 128, :],
            v.stream[:, dc * 2048 : (dc + 1) * 2048], sem="so",
        )
        st.append(d)
    kb.wait("sync", st)

    with stack:
        kb.sem_h = {
            name: stack.enter_context(nc.semaphore(name)) for name in kb.cnt
        }
        with nc.Block() as block:

            @block.sync
            def _(sync):
                for fn in kb.sched["sync"]:
                    fn(sync)

            @block.gpsimd
            def _(gpsimd):
                for fn in kb.sched["gpsimd"]:
                    fn(gpsimd)

            @block.tensor
            def _(tensor):
                for fn in kb.sched["tensor"]:
                    fn(tensor)

            @block.vector
            def _(vector):
                for fn in kb.sched["vector"]:
                    fn(vector)

            @block.scalar
            def _(scalar):
                for fn in kb.sched["scalar"]:
                    fn(scalar)

    return nc, dram


def emit_quant(kb, v, out_bf, in_f32, tmp_f32, scale, lo, hi, deps=None, relu=False):
    kb.wait("scalar", deps)
    s1 = kb.op(
        "scalar",
        lambda e: e.activation(tmp_f32, in_f32, AF.Identity, bias=v.magic[:, 0:1], scale=scale),
        "act",
    )
    kb.wait("vector", s1)
    lo_eff = 0.0 if relu else float(lo)
    s2 = kb.op(
        "vector",
        lambda e: e.tensor_scalar(tmp_f32, tmp_f32, MAGIC, lo_eff, ALU.subtract, ALU.max),
        "dve",
    )
    s3 = kb.op(
        "vector",
        lambda e: e.tensor_scalar(out_bf, tmp_f32, float(hi), 1.0 / scale, ALU.min, ALU.mult),
        "dve",
    )
    return s3


def emit_l0_ffn(kb, v, dram, ps, x_dep):
    cv = v.cv
    A = OFF_PHASE
    AW = cv.bf(A, 32768)
    WF = cv.f32(A + 65536, 4096)
    XQ = cv.bf(A + 81920, 4096)
    HL = cv.bf(A + 90112, 16384)
    HB = cv.bf(A + 122880, 1024)
    TQ = cv.f32(A + 124928, 1024)
    ps["lnbuf0"] = cv.bf(A + 129024, 512)
    ps["lnbuf1"] = cv.bf(A + 130048, 512)
    ps["lntmp"] = cv.f32(A + 124928, 512)
    bank = ps["b4"]
    hs = dram["hscratch"]

    b1 = v.small[:, 0:32]
    b2 = v.small[:, 32:40]
    b1q = v.small[:, 64:96]
    b2q = v.small[:, 128:136]
    bias1 = v.small[:, 192:224]
    d1 = kb.dma("sync", b1, dram["l0_b1c"], sem="lb0")
    d2 = kb.dma("sync", b2, dram["l0_b2c"], sem="lb0")
    qd = emit_quant(
        kb, v, TQ[:, 512:1024], v.small[:, 0:512], TQ[:, 0:512],
        64.0, -128, 127, deps=[d1, d2],
    )
    kb.wait("vector", qd)
    q1 = kb.op("vector", lambda e: e.tensor_copy(b1q, TQ[:, 512:544]), "dve")
    q2 = kb.op("vector", lambda e: e.tensor_copy(b2q, TQ[:, 544:552]), "dve")
    qb = kb.op(
        "vector",
        lambda e: e.tensor_scalar(TQ[:, 0:512], TQ[:, 512:1024], 16.0, MAGIC, ALU.mult, ALU.add),
        "dve",
    )
    kb.wait("vector", qb)
    q3 = kb.op("vector", lambda e: e.tensor_copy(bias1, TQ[:, 0:32]), "dve")

    if "dbg_sm" in dram:
        kb.wait("sync", q3)
        kb.dma("sync", dram["dbg_sm"], v.small, sem="dg")
    w1_deps = []
    prevh = [None, None]
    for i in range(2 * DC):
        dc, half = i // 2, i % 2
        slot = i % 2
        ld = kb.dma(
            "sync", WF[:, slot * 2048 : (slot + 1) * 2048],
            dram["l0_w1T"][dc * 128 : (dc + 1) * 128, half * 2048 : (half + 1) * 2048],
            deps=prevh[slot], sem=f"lw{slot}",
        )
        qd = emit_quant(
            kb, v, AW[:, dc * 4096 + half * 2048 : dc * 4096 + half * 2048 + 2048],
            WF[:, slot * 2048 : (slot + 1) * 2048],
            WF[:, slot * 2048 : (slot + 1) * 2048], 64.0, -128, 127, deps=ld,
        )
        prevh[slot] = qd
        w1_deps.append(qd)

    xqtmp = cv.f32(A + 90112, 512)
    h_store = []
    prev_hdma = [None, None]
    prev_psum = [None, None, None, None]
    prev_xq = None
    for tt in range(NT):
        xq_deps = []
        qprev = None
        for dc in range(DC):
            sl = v.stream[:, dc * 2048 + tt * 512 : dc * 2048 + tt * 512 + 512]
            kb.wait("scalar", x_dep)
            kb.wait("scalar", prev_xq)
            qd = emit_quant(
                kb, v, XQ[:, dc * 512 : (dc + 1) * 512], sl, xqtmp,
                16.0, -128, 127, deps=qprev,
            )
            qprev = qd
            xq_deps.append(qd)
        if tt == 0 and "dbg_xq" in dram:
            kb.dma("sync", dram["dbg_xq"], XQ, deps=xq_deps, sem="dg")
            kb.dma("sync", dram["dbg_w1"], AW, deps=w1_deps, sem="dg")
        for oc in range(HC):
            bi = oc % 4
            b2i = oc % 2
            pb = bank[bi]
            kb.wait("tensor", w1_deps)
            kb.wait("tensor", xq_deps)
            kb.wait("tensor", prev_psum[bi])
            mm = None
            for dc in range(DC):
                mm = kb.op(
                    "tensor",
                    lambda e, o=pb, l=AW[:, dc * 4096 + oc * 128 : dc * 4096 + oc * 128 + 128],
                    r=XQ[:, dc * 512 : (dc + 1) * 512],
                    s=(dc == 0), t=(dc == DC - 1): e.matmul(o, l, r, start=s, stop=t),
                    "pe",
                )
            kb.wait("scalar", mm)
            kb.wait("scalar", prev_hdma[b2i])
            s1 = kb.op(
                "scalar",
                lambda e, i=pb, o=TQ[:, b2i * 512 : b2i * 512 + 512],
                b=bias1[:, oc : oc + 1]: e.activation(
                    o, i, AF.Identity, bias=b, scale=16.0
                ),
                "act",
            )
            kb.wait("vector", s1)
            s2 = kb.op(
                "vector",
                lambda e, o=TQ[:, b2i * 512 : b2i * 512 + 512]: e.tensor_scalar(
                    o, o, MAGIC, 0.0, ALU.subtract, ALU.max
                ),
                "dve",
            )
            s3 = kb.op(
                "vector",
                lambda e, o=HB[:, b2i * 512 : b2i * 512 + 512],
                i=TQ[:, b2i * 512 : b2i * 512 + 512]: e.tensor_scalar(
                    o, i, 127.0, 1.0 / 16.0, ALU.min, ALU.mult
                ),
                "dve",
            )
            hd = kb.dma(
                "sync", hs[oc * 128 : (oc + 1) * 128, tt * 512 : tt * 512 + 512],
                HB[:, b2i * 512 : b2i * 512 + 512], deps=s3, sem=f"hs{b2i}",
            )
            prev_hdma[b2i] = hd
            prev_psum[bi] = s1
            h_store.append(hd)
        prev_xq = kb.now("pe")

    mm1_done = kb.now("pe")
    w2_deps = []
    prev_q = w1_deps[-1]
    for hc in range(HC):
        wslot = hc % 4
        ld = kb.dma(
            "sync", WF[:, wslot * 1024 : (wslot + 1) * 1024],
            dram["l0_w2T"][hc * 128 : (hc + 1) * 128, :], deps=prev_q,
            sem=f"lw{wslot}",
        )
        kb.wait("scalar", mm1_done)
        qd = emit_quant(
            kb, v, AW[:, hc * 1024 : (hc + 1) * 1024],
            WF[:, wslot * 1024 : (wslot + 1) * 1024],
            WF[:, wslot * 1024 : (wslot + 1) * 1024], 64.0, -128, 127, deps=ld,
        )
        if hc >= 3:
            prev_q = w2_deps[hc - 3]
        w2_deps.append(qd)

    if "dbg_w2" in dram:
        kb.dma("sync", dram["dbg_w2"], AW, deps=w2_deps, sem="dg")
    res_deps = []
    prev_ep = [prev_psum[0], prev_psum[1], prev_psum[2], prev_psum[3]]
    hl_prev = [kb.now("hs0"), kb.now("hs1")]
    for tt in range(NT):
        hl_deps = []
        for hc in range(HC):
            d = kb.dma(
                "sync", HL[:, hc * 512 : (hc + 1) * 512],
                hs[hc * 128 : (hc + 1) * 128, tt * 512 : tt * 512 + 512],
                deps=hl_prev, sem=f"hr{tt % 2}",
            )
            hl_deps.append(d)
        for dc in range(DC):
            bi = dc % 4
            pb = bank[bi]
            kb.wait("tensor", w2_deps)
            kb.wait("tensor", hl_deps)
            kb.wait("tensor", prev_ep[bi])
            mm = None
            for hc in range(HC):
                mm = kb.op(
                    "tensor",
                    lambda e, o=pb, l=AW[:, hc * 1024 + dc * 128 : hc * 1024 + dc * 128 + 128],
                    r=HL[:, hc * 512 : (hc + 1) * 512],
                    s=(hc == 0), t=(hc == HC - 1): e.matmul(o, l, r, start=s, stop=t),
                    "pe",
                )
            sl = v.stream[:, dc * 2048 + tt * 512 : dc * 2048 + tt * 512 + 512]
            kb.wait("vector", mm)
            kb.wait("vector", x_dep)
            ep = kb.op(
                "vector",
                lambda e, s=sl, i=pb, b=b2q[:, dc : dc + 1]: e.scalar_tensor_tensor(
                    s, i, b, s, ALU.add, ALU.add
                ),
                "dve",
            )
            prev_ep[bi] = ep
            res_deps.append(ep)
        hl_prev = kb.now("pe")

    if "dbg_r" in dram:
        for dc in range(DC):
            kb.dma("sync", dram["dbg_r"][dc * 128 : (dc + 1) * 128, :],
                   v.stream[:, dc * 2048 : (dc + 1) * 2048], deps=res_deps, sem="dg")
    return emit_ln(kb, v, ps, res_deps)


def rev_ap(ap):
    dims = [list(p) for p in ap.ap]
    assert len(dims) == 2 and dims[1][0] == 1
    (pstep, pcnt), (_, fcnt) = dims
    return bass.AP(ap.tensor, ap.offset + (fcnt - 1), [[pstep, pcnt], [-1, fcnt]])


def emit_l1_attn(kb, v, dram, ps, x_dep):
    cv = v.cv
    A = OFF_PHASE
    x2bf = cv.bf(A, 16384)
    vtm1 = cv.bf(A + 32768, 16384)
    yfm = cv.bf(A + 65536, 16384)
    z_sb = cv.f32(A + 65536, 2048)
    ez = cv.f32(A + 65536 + 8192, 2048)
    zp_t = cv.bf(A + 65536 + 16384, 2048)
    zq = cv.bf(A + 65536 + 20480, 2560)
    toep = [cv.bf(A + 98304, 2432), cv.bf(A + 98304 + 4864, 2432)]
    ws1 = cv.bf(A + 108032, 8192)
    waT = cv.bf(A + 124416, 128)
    ps["lnbuf0"] = cv.bf(A + 124672, 512)
    ps["lnbuf1"] = cv.bf(A + 125696, 512)
    ps["lntmp"] = cv.f32(A + 126720, 512)
    p5, p6 = ps["p5"], ps["p6"]
    bank = ps["b4"]
    zbank = [p5, p6]
    wrev = dram["wrev"]

    wv_deps = []
    for dc in range(DC):
        wv_deps.append(
            kb.dma("gpsimd", ws1[:, dc * 1024 : (dc + 1) * 1024],
                   dram["l1_wvT"][dc * 128 : (dc + 1) * 128, :], sem="gb")
        )
    wa_deps = []
    for dc in range(DC):
        wa_deps.append(
            kb.dma("gpsimd", waT[:, dc * 16 : (dc + 1) * 16],
                   dram["l1_waT"][dc * 128 : (dc + 1) * 128, :], sem="gw")
        )
    bO = v.small[:, 0:8]
    bo_dep = kb.dma("sync", bO, dram["l1_bOc"], sem="lb1")

    xb_deps = []
    for dc in range(DC):
        sl = v.stream[:, dc * 2048 : (dc + 1) * 2048]
        kb.wait("vector", x_dep)
        xb_deps.append(kb.op(
            "vector",
            lambda e, o=x2bf[:, dc * 2048 : (dc + 1) * 2048], i=sl: e.tensor_copy(o, i),
            "dve",
        ))

    prev_evac = [None, None, None, None]
    vtm_deps = []
    for pt in range(16):
        for fh in range(2):
            bi = (pt * 2 + fh) % 4
            pb = bank[bi]
            kb.wait("tensor", xb_deps)
            kb.wait("tensor", wv_deps)
            kb.wait("tensor", prev_evac[bi])
            mm = None
            for dc in range(DC):
                mm = kb.op(
                    "tensor",
                    lambda e, o=pb, l=x2bf[:, dc * 2048 + pt * 128 : dc * 2048 + pt * 128 + 128],
                    r=ws1[:, dc * 1024 + fh * 512 : dc * 1024 + fh * 512 + 512],
                    s=(dc == 0), t=(dc == DC - 1): e.matmul(o, l, r, start=s, stop=t),
                    "pe",
                )
            out_ap = vtm1[:, pt * 1024 + fh * 512 : pt * 1024 + fh * 512 + 512]
            kb.wait("vector", mm)
            ev = kb.op("vector", lambda e, o=out_ap, i=pb: e.tensor_copy(o, i), "dve")
            prev_evac[bi] = ev
            vtm_deps.append(ev)

    z_deps = []
    for tt in range(NT):
        bi = tt % 4
        pb = bank[bi]
        kb.wait("tensor", wa_deps)
        kb.wait("tensor", prev_evac[bi])
        mm = None
        for dc in range(DC):
            mm = kb.op(
                "tensor",
                lambda e, o=pb[0:16, :], l=waT[:, dc * 16 : (dc + 1) * 16],
                r=x2bf[:, dc * 2048 + tt * 512 : dc * 2048 + tt * 512 + 512],
                s=(dc == 0), t=(dc == DC - 1): e.matmul(o, l, r, start=s, stop=t),
                "pe",
            )
        kb.wait("vector", mm)
        ev = kb.op(
            "vector",
            lambda e, o=z_sb[0:16, tt * 512 : tt * 512 + 512], i=pb[0:16, :]:
            e.tensor_scalar_mul(o, i, 0.125),
            "dve",
        )
        prev_evac[bi] = ev
        z_deps.append(ev)

    mrow = v.small[0:16, 256:257]
    srow = v.small[0:16, 288:289]
    kb.wait("vector", z_deps)
    s1 = kb.op(
        "vector",
        lambda e: e.reduce_max(mrow, z_sb[0:16, 0:2048], axis=AX.X, negate=True),
        "dve",
    )
    kb.wait("scalar", s1)
    s2 = kb.op(
        "scalar",
        lambda e: e.activation(ez[0:16, 0:2048], z_sb[0:16, 0:2048], AF.Exp,
                               bias=mrow, scale=1.0),
        "act",
    )
    kb.wait("vector", s2)
    s3 = kb.op(
        "vector",
        lambda e: e.reduce_sum(srow, ez[0:16, 0:2048], axis=AX.X),
        "dve",
    )
    kb.wait("vector", s3)
    s4 = kb.op("vector", lambda e: e.reciprocal(srow, srow), "dve")
    kb.wait("vector", s4)
    s5 = kb.op(
        "vector",
        lambda e: e.tensor_scalar_mul(zp_t[0:16, 0:2048], ez[0:16, 0:2048], srow),
        "dve",
    )
    kb.wait("vector", s5)
    s6 = kb.op(
        "vector",
        lambda e: e.tensor_copy(zq[0:16, 0:2048], rev_ap(zp_t[0:16, 0:2048])),
        "dve",
    )
    s7 = kb.op("vector", lambda e: e.memset(zq[0:16, 2048:2560], 0.0), "dve")
    wz = kb.dma("sync", wrev, zq[0:16, 0:2560], deps=[s6, s7], sem="sz")

    yfm_deps = []
    tp_dep = [None, None]
    tp_ld = [None, None]
    prev_yev = [None, None]
    for head in range(NH):
        tb = head % 2
        tdeps = [kb.now("sz")]
        if tp_dep[tb] is not None:
            tdeps.append(tp_dep[tb])
        tld = kb.dma(
            "sync", toep[tb],
            bass.AP(wrev.tensor, head * 2560, [[1, 128], [1, 2432]]),
            deps=tdeps,
            sem=f"tp{tb}",
        )
        tp_ld[tb] = tld
        fc = head // 2
        r0 = (head % 2) * 64
        for tpp in range(NT):
            nch = NH - 4 * tpp
            zb = (head * NT + tpp) % 2
            pz = zbank[zb]
            kb.wait("tensor", tld)
            kb.wait("tensor", vtm_deps)
            kb.wait("tensor", prev_yev[zb])
            mm = None
            for c in range(nch):
                u0 = 512 * tpp + 128 * c
                mm = kb.op(
                    "tensor",
                    lambda e, o=pz[0:64, :], l=vtm1[:, c * 1024 + head * 64 : c * 1024 + head * 64 + 64],
                    r=toep[tb][:, u0 : u0 + 512],
                    s=(c == 0), t=(c == nch - 1): e.matmul(o, l, r, start=s, stop=t),
                    "pe",
                )
            out_sl = yfm[r0 : r0 + 64,
                         fc * 2048 + 512 * (3 - tpp) : fc * 2048 + 512 * (4 - tpp)]
            kb.wait("vector", mm)
            yev = kb.op(
                "vector",
                lambda e, o=rev_ap(out_sl), i=pz[0:64, :]: e.tensor_copy(o, i),
                "dve",
            )
            prev_yev[zb] = yev
            yfm_deps.append(yev)
        tp_dep[tb] = kb.now("pe")

    kb.wait("gpsimd", kb.now("pe"))
    wo_deps = []
    for fc2 in range(DC):
        wo_deps.append(
            kb.dma("gpsimd", ws1[:, fc2 * 1024 : (fc2 + 1) * 1024],
                   dram["l1_woT"][fc2 * 128 : (fc2 + 1) * 128, :], sem="gb")
        )
    res_deps = []
    for dc in range(DC):
        for qt in range(NT):
            bi = (dc * NT + qt) % 4
            pb = bank[bi]
            kb.wait("tensor", wo_deps)
            kb.wait("tensor", yfm_deps)
            kb.wait("tensor", prev_evac[bi])
            mm = None
            for fc2 in range(DC):
                mm = kb.op(
                    "tensor",
                    lambda e, o=pb,
                    l=ws1[:, fc2 * 1024 + dc * 128 : fc2 * 1024 + dc * 128 + 128],
                    r=yfm[:, fc2 * 2048 + qt * 512 : fc2 * 2048 + qt * 512 + 512],
                    s=(fc2 == 0), t=(fc2 == DC - 1): e.matmul(o, l, r, start=s, stop=t),
                    "pe",
                )
            sl = stream_tile(v, dc, qt)
            kb.wait("vector", mm)
            kb.wait("vector", bo_dep)
            ep = kb.op(
                "vector",
                lambda e, s=sl, i=pb, b=bO[:, dc : dc + 1]: e.scalar_tensor_tensor(
                    s, i, b, s, ALU.add, ALU.add
                ),
                "dve",
            )
            prev_evac[bi] = ep
            res_deps.append(ep)

    kb.barrier()
    return emit_ln(kb, v, ps, res_deps)


def emit_l1_ffn(kb, v, dram, ps, x_dep):
    cv = v.cv
    A = OFF_PHASE
    AW = cv.bf(A, 32768)
    XQb = cv.bf(A + 65536, 4096)
    GB = cv.bf(A + 73728, 1024)
    GL = cv.bf(A + 75776, 16384)
    B0 = A + 75776
    stg = [cv.bf(B0 + 3072 * i, 1536) for i in range(4)]
    zb = [cv.bf(B0 + 12288 + 256 * i, 128) for i in range(4)]
    wld = [cv.bf(B0 + 13312 + 2112 * i, 1028) for i in range(4)]
    wev = [cv.bf(B0 + 21760, 1028), cv.bf(B0 + 23872, 1028)]
    bcoll = cv.bf(B0 + 25984, 32)
    btr = cv.bf(B0 + 26080, 128)
    ps["lnbuf0"] = cv.bf(A + 108544, 512)
    ps["lnbuf1"] = cv.bf(A + 109568, 512)
    ps["lntmp"] = cv.f32(A + 110592, 512)
    psBig = ps["big"]
    pAB, pCD = ps["pAB"], ps["pCD"]
    p5 = ps["p5"]
    pT = ps["pT"]
    bank = ps["b4"]
    cscr1, cscr2, vscr, bscr, gs = (
        dram["cscr1"], dram["cscr2"], dram["vscr"], dram["bscr"], dram["gscratch"]
    )

    def stage_loads(j0, stk_ap, nstages, sidx):
        ss = (j0 // 2) % 2 * 2
        out = [None, None]
        for par in (0, 1):
            jj = j0 + par
            sl = ss + par
            src = stk_ap[jj, :, sidx * 128 : (sidx + nstages) * 128]
            out[par] = kb.dma(
                "gpsimd", stg[sl][:, 0 : nstages * 128], src,
                deps=self_guard[sl], sem=f"gs{sl}",
            )
        return out

    def chain_pair(j0, stk_ap, nstages, w_cols_in, wcols, sidx, sdeps, hasbias):
        ss = (j0 // 2) % 2 * 2
        zcur = [stg[ss][:, 0:128], stg[ss + 1][:, 0:128]]
        pdep = [chain_guard[0], chain_guard[1]]
        prod_ps = [pCD[:, 0:128], pCD[:, 512:640]]
        for i in range(1, nstages):
            for par in (0, 1):
                kb.wait("tensor", sdeps[par])
                kb.wait("tensor", pdep[par])
                mm = kb.op(
                    "tensor",
                    lambda e, l=stg[ss + par][:, i * 128 : (i + 1) * 128], r=zcur[par],
                    o=prod_ps[par]: e.matmul(o, l, r, start=True, stop=True),
                    "pe",
                )
                nz = zb[par * 2 + (i % 2)]
                kb.wait("vector", mm)
                ev = kb.op(
                    "vector",
                    lambda e, o=nz, i2=prod_ps[par]: e.tensor_copy(o, i2),
                    "dve",
                )
                pdep[par] = ev
                zcur[par] = nz
        self_guard[ss] = self_guard[ss + 1] = kb.now("pe")
        evs = [None, None]
        app_mm = [None, None]
        for par in (0, 1):
            kb.wait("tensor", pdep[par])
            kb.wait("tensor", wload_guard[par])
            kb.wait("tensor", apply_guard[par])
            mm = None
            wsl = wld[ss + par]
            if par == 0:
                for c0 in range(0, min(wcols, 1024), 512):
                    mm = kb.op(
                        "tensor",
                        lambda e, l=zcur[0], r=wsl[:, c0 : c0 + 512],
                        o=psBig[:, c0 : c0 + 512]: e.matmul(o, l, r, start=True, stop=True),
                        "pe",
                    )
                if hasbias:
                    mm = kb.op(
                        "tensor",
                        lambda e, l=zcur[0], r=wsl[:, 1024:1025],
                        o=psBig[:, 1024:1025]: e.matmul(o, l, r, start=True, stop=True),
                        "pe",
                    )
            else:
                for c0 in range(0, min(wcols, 1024), 512):
                    mm = kb.op(
                        "tensor",
                        lambda e, l=zcur[1], r=wsl[:, c0 : c0 + 512],
                        o=pAB[:, c0 : c0 + 512]: e.matmul(o, l, r, start=True, stop=True),
                        "pe",
                    )
                if hasbias:
                    mm = kb.op(
                        "tensor",
                        lambda e, l=zcur[1], r=wsl[:, 1024:1025],
                        o=p5[:, 0:1]: e.matmul(o, l, r, start=True, stop=True),
                        "pe",
                    )
            app_mm[par] = mm
        for par in (0, 1):
            kb.wait("vector", app_mm[par])
            kb.wait("vector", wev_guard[par])
            if par == 0:
                ev = kb.op(
                    "vector",
                    lambda e: e.tensor_copy(wev[0][:, 0:wcols], psBig[:, 0:wcols]),
                    "dve",
                )
            else:
                ev = kb.op(
                    "vector",
                    lambda e: e.tensor_copy(wev[1][:, 0:1024], pAB[:, 0:1024]),
                    "dve",
                )
                if hasbias:
                    ev = kb.op(
                        "vector",
                        lambda e: e.tensor_copy(wev[1][:, 1024:1025], p5[:, 0:1]),
                        "dve",
                    )
            evs[par] = ev
            apply_guard[par] = ev
            chain_guard[par] = pdep[par]
            wld_free[ss + par] = app_mm[par]
        return evs

    self_guard = [None, None, None, None]
    chain_guard = [None, None]
    apply_guard = [None, None]
    wload_guard = [None, None]
    wev_guard = [None, None]
    wld_free = [None, None, None, None]

    st_up = dram["l1_upstk"]

    def p1_loads(j):
        ss = (j // 2) % 2 * 2
        sd = stage_loads(j, st_up, 7, 0)
        wg = [None, None]
        for par in (0, 1):
            wg[par] = kb.dma(
                "gpsimd", wld[ss + par][:, 0:1025],
                dram["l1_wexpA"][(j + par) * 128 : (j + par + 1) * 128, :],
                deps=wld_free[ss + par], sem=f"gw{ss + par}",
            )
        return sd, wg

    pending = p1_loads(0)
    for j in range(0, HC, 2):
        sd, wg = pending
        wload_guard[0], wload_guard[1] = wg
        if j + 2 < HC:
            pending = p1_loads(j + 2)
        evs = chain_pair(j, st_up, 7, 1025, 1025, 0, sd, True)
        for par in (0, 1):
            st = kb.dma(
                "sync", cscr1[(j + par) * 128 : (j + par + 1) * 128, 0:1025],
                wev[par][:, 0:1025], deps=evs[par], sem=f"s1{par}",
            )
            wev_guard[par] = st

    c1r = cscr1.rearrange("(c p) n -> c p n", c=32)
    c2r = cscr2.rearrange("(c p) n -> c p n", c=32)
    bc_deps = []

    def p2_loads(j):
        ss = (j // 2) % 2 * 2
        sd = stage_loads(j, st_up, 5, 7)
        wg = [None, None]
        for par in (0, 1):
            jj = j + par
            ldp = [kb.now("s10"), kb.now("s11")]
            if wld_free[ss + par] is not None:
                ldp.append(wld_free[ss + par])
            wds = []
            for a in range(4):
                wds.append(kb.dma(
                    "sync", wld[ss + par][32 * a : 32 * a + 32, 0:1025],
                    c1r[:, 4 * jj + a, 0:1025], deps=ldp, sem=f"ga{ss + par}",
                ))
            wg[par] = wds
        return sd, wg

    pending = p2_loads(0)
    for j in range(0, HC, 2):
        sd, wg = pending
        wload_guard[0], wload_guard[1] = wg
        if j + 2 < HC:
            pending = p2_loads(j + 2)
        evs = chain_pair(j, st_up, 5, 1025, 1025, 7, sd, True)
        for par in (0, 1):
            jj = j + par
            kb.wait("vector", evs[par])
            bc = kb.op(
                "vector",
                lambda e, o=bcoll[:, jj : jj + 1], i=wev[par][:, 1024:1025]:
                e.tensor_copy(o, i),
                "dve",
            )
            bc_deps.append(bc)
            sts = []
            for a in range(4):
                sts.append(kb.dma(
                    "sync", c2r[:, 4 * jj + a, 0:1024],
                    wev[par][32 * a : 32 * a + 32, 0:1024], deps=evs[par],
                    sem=f"s2{par}",
                ))
            wev_guard[par] = sts

    kb.wait("tensor", bc_deps)
    bt = kb.op(
        "tensor",
        lambda e: e.transpose(pT[0:32, 0:128], bcoll[:, 0:32], v.id128),
        "pe",
    )
    kb.wait("vector", bt)
    bt2 = kb.op("vector", lambda e: e.tensor_copy(btr[0:32, :], pT[0:32, 0:128]), "dve")
    bt3 = kb.dma("sync", bscr, btr[0:32, 0:128], deps=bt2, sem="sb")
    borig = v.small[:, 64:96]
    ubc = v.small[:, 96:128]
    badd = v.small[:, 160:192]
    bd1 = kb.dma("gpsimd", borig, bass.AP(bscr.tensor, 0, [[32, 128], [1, 32]]),
                 deps=kb.now("sb"), sem="gc2")
    bd2 = kb.dma("sync", ubc, dram["l1_ubc"], sem="lb2")
    kb.wait("vector", [bd1, bd2])
    bd3 = kb.op("vector", lambda e: e.tensor_add(badd, borig, ubc), "dve")

    tw_deps = []
    for dc in range(DC):
        kb.wait("sync", [kb.now("s20"), kb.now("s21")])
        tw_deps.append(kb.op(
            "sync",
            lambda e, o=AW[:, dc * 4096 : (dc + 1) * 4096],
            i=cscr2[:, dc * 128 : (dc + 1) * 128]: e.dma_start_transpose(o, i),
            "tw", 16,
        ))

    st_dn = dram["l1_dnstk"]
    p5war = [kb.now("ga0"), kb.now("ga1")]

    def p5_loads(j):
        ss = (j // 2) % 2 * 2
        sd = stage_loads(j, st_dn, 5, 0)
        wg = [None, None]
        for par in (0, 1):
            wg[par] = kb.dma(
                "gpsimd", wld[ss + par][:, 0:1024],
                dram["l1_wconP"][(j + par) * 128 : (j + par + 1) * 128, :],
                deps=wld_free[ss + par], sem=f"gw{ss + par}",
            )
        return sd, wg

    pending = p5_loads(0)
    for j in range(0, HC, 2):
        sd, wg = pending
        wload_guard[0], wload_guard[1] = wg
        if j + 2 < HC:
            pending = p5_loads(j + 2)
        evs = chain_pair(j, st_dn, 5, 1024, 1024, 0, sd, False)
        for par in (0, 1):
            st = kb.dma(
                "sync", cscr1[(j + par) * 128 : (j + par + 1) * 128, 0:1024],
                wev[par][:, 0:1024], deps=[evs[par]] + p5war, sem=f"s3{par}",
            )
            wev_guard[par] = st

    def p6_loads(j):
        ss = (j // 2) % 2 * 2
        sd = stage_loads(j, st_dn, 7, 5)
        wg = [None, None]
        for par in (0, 1):
            jj = j + par
            ldp = [kb.now("s30"), kb.now("s31")]
            if wld_free[ss + par] is not None:
                ldp.append(wld_free[ss + par])
            wg[par] = kb.dma(
                "sync", wld[ss + par][:, 0:1024],
                bass.AP(cscr1.tensor, jj * (D + 2), [[32 * (D + 2), 128], [1, 1024]]),
                deps=ldp, sem=f"ga{ss + par}",
            )
        return sd, wg

    pending = p6_loads(0)
    for j in range(0, HC, 2):
        sd, wg = pending
        wload_guard[0], wload_guard[1] = wg
        if j + 2 < HC:
            pending = p6_loads(j + 2)
        evs = chain_pair(j, st_dn, 7, 1024, 1024, 5, sd, False)
        for par in (0, 1):
            st = kb.dma(
                "sync", vscr[(j + par) * 128 : (j + par + 1) * 128, :],
                wev[par][:, 0:1024], deps=evs[par], sem=f"sv{par}",
            )
            wev_guard[par] = st

    g_store = []
    prev_gdma = [None, None]
    prev_psum7 = [None, None, None, None]
    prev_xq = None
    for tt in range(NT):
        xb_deps = []
        for dc in range(DC):
            sl = v.stream[:, dc * 2048 + tt * 512 : dc * 2048 + tt * 512 + 512]
            kb.wait("vector", x_dep)
            kb.wait("vector", prev_xq)
            xb_deps.append(kb.op(
                "vector",
                lambda e, o=XQb[:, dc * 512 : (dc + 1) * 512], i=sl: e.tensor_copy(o, i),
                "dve",
            ))
        for oc in range(HC):
            bi = oc % 4
            b2i = oc % 2
            pb = bank[bi]
            kb.wait("tensor", tw_deps)
            kb.wait("tensor", xb_deps)
            kb.wait("tensor", prev_psum7[bi])
            mm = None
            for dc in range(DC):
                mm = kb.op(
                    "tensor",
                    lambda e, o=pb, l=AW[:, dc * 4096 + oc * 128 : dc * 4096 + oc * 128 + 128],
                    r=XQb[:, dc * 512 : (dc + 1) * 512],
                    s=(dc == 0), t=(dc == DC - 1): e.matmul(o, l, r, start=s, stop=t),
                    "pe",
                )
            kb.wait("scalar", mm)
            kb.wait("scalar", prev_gdma[b2i])
            kb.wait("scalar", bd3)
            g1 = kb.op(
                "scalar",
                lambda e, i=pb, o=GB[:, b2i * 512 : b2i * 512 + 512],
                b=badd[:, oc : oc + 1]: e.activation(o, i, AF.Gelu, bias=b, scale=1.0),
                "act",
            )
            gd = kb.dma(
                "sync", gs[oc * 128 : (oc + 1) * 128, tt * 512 : tt * 512 + 512],
                GB[:, b2i * 512 : b2i * 512 + 512], deps=g1, sem=f"hs{b2i}",
            )
            prev_gdma[b2i] = gd
            prev_psum7[bi] = g1
            g_store.append(gd)
        prev_xq = kb.now("pe")

    mm1_done = kb.now("pe")
    v_deps = []
    prev_v = None
    for hc in range(HC):
        kb.wait("sync", mm1_done)
        d = kb.dma(
            "sync", AW[:, hc * 1024 : (hc + 1) * 1024],
            vscr[hc * 128 : (hc + 1) * 128, :],
            deps=[kb.now("sv0"), kb.now("sv1")], sem=f"lw{hc % 4}",
        )
        v_deps.append(d)

    bcc = v.small[:, 200:208]
    dbc = v.small[:, 208:216]
    bsum = v.small[:, 224:232]
    k1 = kb.dma("sync", bcc, dram["l1_bcc"], sem="lb3")
    k2 = kb.dma("sync", dbc, dram["l1_dbc"], sem="lb3")
    kb.wait("vector", [k1, k2])
    k3 = kb.op("vector", lambda e: e.tensor_add(bsum, bcc, dbc), "dve")
    res_deps = []
    prev_ep = [prev_psum7[0], prev_psum7[1], prev_psum7[2], prev_psum7[3]]
    gl_prev = [kb.now("hs0"), kb.now("hs1")]
    for tt in range(NT):
        gl_deps = []
        for hc in range(HC):
            d = kb.dma(
                "sync", GL[:, hc * 512 : (hc + 1) * 512],
                gs[hc * 128 : (hc + 1) * 128, tt * 512 : tt * 512 + 512],
                deps=gl_prev, sem=f"hr{tt % 2}",
            )
            gl_deps.append(d)
        for dc in range(DC):
            bi = dc % 4
            pb = bank[bi]
            kb.wait("tensor", v_deps)
            kb.wait("tensor", gl_deps)
            kb.wait("tensor", prev_ep[bi])
            mm = None
            for hc in range(HC):
                mm = kb.op(
                    "tensor",
                    lambda e, o=pb, l=AW[:, hc * 1024 + dc * 128 : hc * 1024 + dc * 128 + 128],
                    r=GL[:, hc * 512 : (hc + 1) * 512],
                    s=(hc == 0), t=(hc == HC - 1): e.matmul(o, l, r, start=s, stop=t),
                    "pe",
                )
            sl = stream_tile(v, dc, tt)
            kb.wait("vector", mm)
            kb.wait("vector", [x_dep, k3])
            ep = kb.op(
                "vector",
                lambda e, s=sl, i=pb, b=bsum[:, dc : dc + 1]: e.scalar_tensor_tensor(
                    s, i, b, s, ALU.add, ALU.add
                ),
                "dve",
            )
            prev_ep[bi] = ep
            res_deps.append(ep)
        gl_prev = kb.now("pe")

    kb.barrier()
    return emit_ln(kb, v, ps, res_deps)


def host_consts():
    c = np.zeros((128, 260), np.float32)
    r = np.arange(128)
    c[:, 0:128] = (r[:, None] <= np.arange(128)[None, :]).astype(np.float32)
    c[:, 128:256] = np.eye(128, dtype=np.float32)
    return c


def host_prep(inputs, blocks):
    m = {"consts": host_consts()}
    if 0 in blocks:
        m["l0_wkT"] = np.ascontiguousarray(
            inputs["L0_attn_W_K"].reshape(1024, 1024).T
        )
        m["l0_wqT"] = np.ascontiguousarray(
            inputs["L0_attn_W_Q"].reshape(1024, 1024).T
        )
        m["l0_wvT"] = np.ascontiguousarray(
            inputs["L0_attn_W_V"].reshape(1024, 1024).T
        )
        m["l0_woT"] = np.ascontiguousarray(inputs["L0_attn_W_O"].T)
    if 1 in blocks:
        m["l0_w1T"] = np.ascontiguousarray(inputs["L0_ffn_w1"].T)
        m["l0_w2T"] = np.ascontiguousarray(inputs["L0_ffn_w2"].T)
        m["l0_b1c"] = np.ascontiguousarray(inputs["L0_ffn_b1"].reshape(32, 128).T)
        m["l0_b2c"] = np.ascontiguousarray(inputs["L0_ffn_b2"].reshape(8, 128).T)
    if 2 in blocks:
        m["l1_waT"] = np.ascontiguousarray(inputs["L1_attn_W_A"].T)
        m["l1_wvT"] = np.ascontiguousarray(inputs["L1_attn_W_V"].T)
        m["l1_woT"] = np.ascontiguousarray(inputs["L1_attn_W_O"].T)
        m["l1_bOc"] = np.ascontiguousarray(inputs["L1_attn_b_O"].reshape(8, 128).T)
    if 3 in blocks:
        m.update(host_prep_l1ffn(inputs))
    return m


def _scatter_stage(wstage, s, transpose):
    sig = 1 << s
    A = np.zeros((4096, 4096), np.float32)
    k = np.arange(2048)
    i = (k // sig) * 2 * sig + (k % sig)
    A[i, i] = wstage[:, 0, 0]
    A[i, i + sig] = wstage[:, 0, 1]
    A[i + sig, i] = wstage[:, 1, 0]
    A[i + sig, i + sig] = wstage[:, 1, 1]
    if transpose:
        A = A.T
    out = np.zeros((32, 128, 128), np.float32)
    for j in range(32):
        out[j] = A[j * 128 : (j + 1) * 128, j * 128 : (j + 1) * 128]
    return out


def _perm_P():
    f = np.arange(4096)
    c = f // 128
    p = f % 128
    fp = p * 32 + c
    return fp


def _scatter_stage_relab(wstage, s):
    sig = 1 << s
    A = np.zeros((4096, 4096), np.float32)
    k = np.arange(2048)
    i = (k // sig) * 2 * sig + (k % sig)
    A[i, i] = wstage[:, 0, 0]
    A[i, i + sig] = wstage[:, 0, 1]
    A[i + sig, i] = wstage[:, 1, 0]
    A[i + sig, i + sig] = wstage[:, 1, 1]
    fp = _perm_P()
    Ar = np.zeros_like(A)
    Ar[np.ix_(fp, fp)] = A
    return Ar


def _blocks_of(A):
    out = np.zeros((32, 128, 128), np.float32)
    for j in range(32):
        out[j] = A[j * 128 : (j + 1) * 128, j * 128 : (j + 1) * 128]
    return out


def _chain_stack(mats_blocks):
    k = len(mats_blocks)
    out = [np.transpose(mats_blocks[-1], (0, 2, 1))]
    for i in range(k - 2, -1, -1):
        out.append(mats_blocks[i])
    return np.stack(out)


def _pack_stk(stk):
    return np.ascontiguousarray(np.transpose(stk, (1, 2, 0, 3)).reshape(32, 128, 12 * 128))


def host_prep_l1ffn(inputs):
    m = {}
    wexp = inputs["L1_ffn_W_exp"]
    bexp = inputs["L1_ffn_b_exp"]
    m["l1_wexpA"] = np.ascontiguousarray(
        np.concatenate([np.asarray(wexp).T, np.asarray(bexp)[:, None]], axis=1)
    )
    fp = _perm_P()
    wcon = np.asarray(inputs["L1_ffn_W_con"])
    wconP = np.zeros_like(wcon)
    wconP[fp] = wcon
    m["l1_wconP"] = np.ascontiguousarray(wconP)

    upw = np.asarray(inputs["L1_ffn_up_weights"])
    dnw = np.asarray(inputs["L1_ffn_down_weights"])

    up_g1 = [_scatter_stage(upw[s], s, transpose=False) for s in range(7)]
    up_g2 = [_blocks_of(_scatter_stage_relab(upw[s], s)) for s in range(7, 12)]
    st1 = _chain_stack(up_g1)
    st2 = _chain_stack(up_g2)
    m["l1_upstk"] = _pack_stk(np.concatenate([st1, st2], axis=0))

    dn_g2 = [
        np.transpose(_blocks_of(_scatter_stage_relab(dnw[s], s)), (0, 2, 1))
        for s in range(11, 6, -1)
    ]
    dn_g1 = [
        _scatter_stage(dnw[s], s, transpose=True) for s in range(6, -1, -1)
    ]
    st1d = _chain_stack(dn_g2)
    st2d = _chain_stack(dn_g1)
    m["l1_dnstk"] = _pack_stk(np.concatenate([st1d, st2d], axis=0))

    m["l1_ubc"] = np.ascontiguousarray(
        np.asarray(inputs["L1_ffn_up_bias"]).reshape(32, 128).T
    )
    bsum_src = np.stack(
        [np.asarray(inputs["L1_ffn_b_con"]), np.asarray(inputs["L1_ffn_down_bias"])]
    )
    m["l1_bcc"] = np.ascontiguousarray(
        bsum_src.reshape(2, 8, 128).transpose(2, 0, 1).reshape(128, 16)[:, 0:8]
    )
    m["l1_bcc"] = np.ascontiguousarray(
        np.asarray(inputs["L1_ffn_b_con"]).reshape(8, 128).T
    )
    m["l1_dbc"] = np.ascontiguousarray(
        np.asarray(inputs["L1_ffn_down_bias"]).reshape(8, 128).T
    )
    return m


_NC_CACHE = {}


def kernel(**inputs):
    x = np.asarray(inputs["L0_attn_x"])
    ncores = 8
    blocks_all = [b for launch in LAUNCH_PLAN for b in launch]
    common = host_prep(inputs, blocks_all)

    xs = [np.ascontiguousarray(x[b].T) for b in range(ncores)]
    cur = xs
    for launch in LAUNCH_PLAN:
        key = tuple(launch)
        if key not in _NC_CACHE:
            _NC_CACHE[key] = build_program(list(launch))
        nc, dram = _NC_CACHE[key]
        in_maps = []
        for b in range(ncores):
            im = {"xin": cur[b]}
            for k, arr in common.items():
                if k in dram:
                    im[k] = arr
            in_maps.append(im)
        res = run_bass_kernel_spmd(nc, in_maps, core_ids=list(range(ncores)))
        cur = [np.asarray(res.results[b]["xout"]) for b in range(ncores)]

    out = np.stack([c.T for c in cur])
    return np.ascontiguousarray(out.astype(np.float32))
